# version 01
# baseline (speedup 1.0000x reference)
import numpy as np
from contextlib import ExitStack
import concourse.bass as bass
import concourse.mybir as mybir
from concourse.bass_utils import run_bass_kernel_spmd

F32 = mybir.dt.float32
BF16 = mybir.dt.bfloat16
I32 = mybir.dt.int32
U8 = mybir.dt.uint8
AF = mybir.ActivationFunctionType
ALU = mybir.AluOpType

ENGS = ("tensor", "vector", "scalar", "gpsimd", "sync")


class Sem:
    __slots__ = ("h", "n", "name")

    def __init__(self, h, name):
        self.h = h
        self.n = 0
        self.name = name


class Prog:

    def __init__(self, nc, es):
        self.nc = nc
        self.es = es
        self.q = {e: [] for e in ENGS}
        self.waited = {e: {} for e in ENGS}
        self.sems = []

    def sem(self, name):
        s = Sem(self.es.enter_context(self.nc.semaphore(name)), name)
        self.sems.append(s)
        return s

    def raw(self, eng, f):
        self.q[eng].append(f)

    def wait(self, eng, tok):
        if tok is None:
            return
        toks = tok if isinstance(tok, list) else [tok]
        flat = []
        for t in toks:
            if isinstance(t, list):
                flat.extend(t)
            else:
                flat.append(t)
        best = {}
        for t in flat:
            if t is None:
                continue
            s, tgt = t
            if best.get(s, -1) < tgt:
                best[s] = tgt
        for s, tgt in best.items():
            w = self.waited[eng]
            if w.get(s, -1) >= tgt:
                continue
            w[s] = tgt
            self.q[eng].append(lambda e, s=s, tgt=tgt: e.wait_ge(s.h, tgt))

    def ins(self, eng, meth, *args, waits=None, inc=None, **kw):
        self.wait(eng, waits)
        amt = 16 if meth.startswith("dma") or meth == "indirect_dma_start" else 1
        tok = None
        if inc is not None:
            inc.n += amt
            tok = (inc, inc.n)

        def f(e, meth=meth, args=args, kw=kw, inc=inc):
            r = getattr(e, meth)(*args, **kw)
            if inc is not None:
                r.then_inc(inc.h, 16 if meth.startswith("dma") or meth == "indirect_dma_start" else 1)

        self.q[eng].append(f)
        return tok

    def emit(self):
        block = self.es.enter_context(self.nc.Block())
        qs = self.q

        @block.tensor
        def _(e):
            for f in qs["tensor"]:
                f(e)

        @block.vector
        def _(e):
            for f in qs["vector"]:
                f(e)

        @block.scalar
        def _(e):
            for f in qs["scalar"]:
                f(e)

        @block.gpsimd
        def _(e):
            for f in qs["gpsimd"]:
                f(e)

        @block.sync
        def _(e):
            for f in qs["sync"]:
                f(e)


class Bank:

    __slots__ = ("ap", "ev", "name")

    def __init__(self, ap, name):
        self.ap = ap
        self.ev = None
        self.name = name


class Ctx:

    def __init__(self, nc, es, p: Prog):
        self.nc = nc
        self.es = es
        self.p = p
        self.te = p.sem("te")
        self.ve = p.sem("ve")
        self.se = p.sem("se")
        self.gp = p.sem("gp")
        self.ldS = p.sem("ldS")
        self.ldG = p.sem("ldG")
        self.sx = p.sem("sx")
        self.st = p.sem("st")
        self.dma = [p.sem(f"dm{i}") for i in range(8)]
        self.banks = [Bank(es.enter_context(nc.psum_tensor(f"pb{i}", [128, 512], F32)).ap(), f"pb{i}")
                      for i in range(6)]
        self.tbanks = [Bank(es.enter_context(nc.psum_tensor(f"tb{i}", [128, 1024], BF16)).ap(), f"tb{i}")
                       for i in range(2)]

    def sb(self, name, shape, dt):
        return self.es.enter_context(self.nc.sbuf_tensor(name, list(shape), dt)).ap()


def mm(p, ctx, bank, lhsT, rhs, start, stop, inc_on_stop=True, tile_position=None, waits=None):
    kw = {}
    if tile_position is not None:
        kw["tile_position"] = tile_position
    tok = p.ins("tensor", "matmul", bank, lhsT, rhs, start=start, stop=stop,
                waits=waits, inc=(ctx.te if (stop and inc_on_stop) else None), **kw)
    return tok



def lhsT_layout(w):
    return np.ascontiguousarray(w.astype(np.float32))


SCAL_COLS = {"eps": 0, "m10": 1, "m50": 2, "one": 3, "half": 4}
SCAL_VALS = {"eps": 1e-5, "m10": -10.0, "m50": -50.0, "one": 1.0, "half": 0.5}


def make_consts():
    import numpy as _np
    c = {}
    c["c_ident"] = _np.eye(128, dtype=_np.float32)
    tri = _np.tril(_np.ones((128, 128), _np.float32))
    c["c_triL"] = tri
    c["c_ones"] = _np.ones((128, 128), _np.float32)
    sc = _np.zeros((128, 32), _np.float32)
    for k, j in SCAL_COLS.items():
        sc[:, j] = SCAL_VALS[k]
    c["c_scal"] = sc
    oh4 = _np.zeros((4, 512), _np.float32)
    for e in range(4):
        oh4[e, e * 128:(e + 1) * 128] = 1.0
    c["c_oh4"] = oh4
    c["c_iota4"] = _np.tile(_np.arange(4, dtype=_np.float32)[None, :], (128, 1))
    c["c_iota4c"] = _np.tile(_np.arange(4, dtype=_np.float32)[:, None], (1, 4))
    kp = _np.arange(128)[:, None]
    qf = _np.arange(256)[None, :]
    c["c_cm0"] = (kp <= qf).astype(_np.float32)
    c["c_cm1"] = (kp + 128 <= qf).astype(_np.float32)
    c["c_p0"] = _np.tile((kp - qf).astype(_np.float32), (1, 1))
    c["c_p0m0"] = _np.where(kp <= qf, (kp - qf).astype(_np.float32), -1e5).astype(_np.float32)
    c["c_p0m1"] = _np.where(kp + 128 <= qf, (kp - qf + 128).astype(_np.float32), -1e5).astype(_np.float32)
    import math as _math

    def _alibi_slopes(n_heads):
        def pow2(n):
            start = 2.0 ** (-2.0 ** (-(_math.log2(n) - 3)))
            return [start * start ** i for i in range(n)]
        if n_heads & (n_heads - 1) == 0:
            s = pow2(n_heads)
        else:
            cp2 = 2 ** _math.floor(_math.log2(n_heads))
            s = pow2(cp2)
            eb = 2.0 ** (-2.0 ** (-(_math.log2(2 * cp2) - 3)))
            s += [eb * eb ** i for i in range(n_heads - cp2)]
            s = s[:n_heads]
        return _np.array(s, dtype=_np.float32)
    sl = _alibi_slopes(16)
    c["c_aslp"] = _np.tile(sl[None, :], (128, 1)).astype(_np.float32)
    aoff = _np.zeros((128, 16 * 17), _np.float32)
    for h in range(16):
        for jj in range(-1, 16):
            aoff[:, h * 17 + jj + 1] = sl[h] * (-128.0 * jj)
    c["c_aoff"] = aoff
    r64 = _np.arange(128)[:, None] % 64
    q64 = _np.arange(64)[None, :]
    c["c_cmp64"] = (r64 <= q64).astype(_np.float32)
    j32 = _np.arange(32)[:, None]
    i32 = _np.arange(32)[None, :]
    c["c_slt32"] = (j32 < i32).astype(_np.float32)
    c["c_triu"] = tri.T.copy()
    oh8 = _np.zeros((8, 1024), _np.float32)
    for e in range(8):
        oh8[e, e * 128:(e + 1) * 128] = 1.0
    c["c_oh8"] = oh8
    c["c_iota512"] = _np.tile(_np.arange(512, dtype=_np.float32)[None, :], (128, 1))
    c["c_iotac"] = _np.tile(_np.arange(128, dtype=_np.float32)[:, None], (1, 4))
    return c



def load_weight(p, ctx, wbuf_ap, dram_ap, din, dout, queue="gpsimd", sem=None, waits=None):
    k = din // 128
    src = dram_ap.rearrange("(k p) m -> p k m", p=128)
    dst = wbuf_ap.rearrange("p (k m) -> p k m", k=k)
    assert sem is not None
    return p.ins(queue, "dma_start", out=dst, in_=src, waits=waits, inc=sem)


def evac(p, ctx, bank, dst, func=None, bias=None, scale=1.0, add_to=None, eng="vector",
         add_row=None, out_slice=None, mult_by=None):
    src = bank.ap if out_slice is None else out_slice
    if func is not None:
        kw = {}
        if bias is not None:
            kw["bias"] = bias
        if scale != 1.0:
            kw["scale"] = scale
        tok = p.ins("scalar", "activation", dst, src, func, waits=[(ctx.te, ctx.te.n)], inc=ctx.se, **kw)
    elif add_to is not None:
        tok = p.ins("vector", "tensor_tensor", dst, add_to, src, op=ALU.add,
                    waits=[(ctx.te, ctx.te.n)], inc=ctx.ve)
    elif add_row is not None:
        tok = p.ins("vector", "tensor_tensor", dst, src, add_row, op=ALU.add,
                    waits=[(ctx.te, ctx.te.n)], inc=ctx.ve)
    elif mult_by is not None:
        tok = p.ins("vector", "tensor_tensor", dst, src, mult_by, op=ALU.mult,
                    waits=[(ctx.te, ctx.te.n)], inc=ctx.ve)
    else:
        tok = p.ins("vector", "tensor_copy", dst, src, waits=[(ctx.te, ctx.te.n)], inc=ctx.ve)
    bank.ev = tok
    return tok


def layernorm(p, ctx, X, XN, A_bc, B_bc, T, ones_f32, sq_scratch, rows, x_tok, scal=None, tmpbuf=None):
    nt = T // 512
    bS, bQ, bA, bB = ctx.banks[0], ctx.banks[1], ctx.banks[2], ctx.banks[3]
    eps = 1e-5
    last = None
    prev_sums_tok = None
    for it in range(nt):
        sl = slice(it * 512, (it + 1) * 512)
        p.wait("scalar", x_tok)
        p.wait("scalar", prev_sums_tok)
        for c in range(8):
            sq_tok = p.ins("scalar", "activation", sq_scratch[:, c * 512:(c + 1) * 512],
                           X[:, c * T + it * 512: c * T + (it + 1) * 512], AF.Square,
                           inc=(ctx.se if c == 7 else None))
        p.wait("tensor", x_tok)
        p.wait("tensor", bS.ev)
        p.wait("tensor", bQ.ev)
        for c in range(8):
            p.ins("tensor", "matmul", bS.ap[0:1, 0:512], ones_f32[:, 0:1],
                  X[:, c * T + it * 512: c * T + (it + 1) * 512],
                  start=(c == 0), stop=(c == 7), inc=(None if c < 7 else ctx.te))
        p.wait("tensor", sq_tok)
        for c in range(8):
            p.ins("tensor", "matmul", bQ.ap[0:1, 0:512], ones_f32[:, 0:1],
                  sq_scratch[:, c * 512:(c + 1) * 512],
                  start=(c == 0), stop=(c == 7), inc=(None if c < 7 else ctx.te))
        sums_tok = (ctx.te, ctx.te.n)
        prev_sums_tok = sums_tok
        m = rows[0:1, 0 * 512: 1 * 512]
        t1 = rows[0:1, 1 * 512: 2 * 512]
        r = rows[0:1, 2 * 512: 3 * 512]
        mr = rows[0:1, 3 * 512: 4 * 512]
        p.wait("vector", sums_tok)
        p.ins("vector", "tensor_scalar", m, bS.ap[0:1, 0:512], 1.0 / 1024, None, op0=ALU.mult)
        p.ins("vector", "tensor_scalar", t1, bQ.ap[0:1, 0:512], 1.0 / 1024, None, op0=ALU.mult)
        bQ.ev = p.ins("vector", "tensor_tensor", mr, m, m, op=ALU.mult, inc=ctx.ve)
        bS.ev = bQ.ev
        p.ins("vector", "tensor_tensor", t1, t1, mr, op=ALU.subtract, inc=ctx.ve)
        p.wait("scalar", (ctx.ve, ctx.ve.n))
        p.ins("scalar", "activation", r, t1, AF.Sqrt, bias=scal[0:1, SCAL_COLS["eps"]:SCAL_COLS["eps"] + 1], inc=ctx.se)
        p.wait("vector", (ctx.se, ctx.se.n))
        p.ins("vector", "reciprocal", r, r)
        p.ins("vector", "tensor_tensor", mr, m, r, op=ALU.mult, inc=ctx.ve)
        rows_tok = (ctx.ve, ctx.ve.n)
        p.wait("tensor", rows_tok)
        p.wait("tensor", bA.ev)
        p.wait("tensor", bB.ev)
        p.ins("tensor", "matmul", bA.ap[:, 0:512], ones_f32[0:1, :], r, start=True, stop=True, inc=ctx.te)
        p.ins("tensor", "matmul", bB.ap[:, 0:512], ones_f32[0:1, :], mr, start=True, stop=True, inc=ctx.te)
        bc_tok = (ctx.te, ctx.te.n)
        bA.ev = evac(p, ctx, bA, A_bc[:, sl], out_slice=bA.ap[:, 0:512])
        bB.ev = evac(p, ctx, bB, B_bc[:, sl], out_slice=bB.ap[:, 0:512])
        ab_tok = (ctx.ve, ctx.ve.n)
        for c in range(8):
            xs = X[:, c * T + it * 512: c * T + (it + 1) * 512]
            tmp = tmpbuf[:, 0:512]
            p.ins("vector", "tensor_tensor", tmp, xs, A_bc[:, sl], op=ALU.mult)
            last = p.ins("vector", "tensor_tensor", XN[:, c * T + it * 512: c * T + (it + 1) * 512],
                         tmp, B_bc[:, sl], op=ALU.subtract, inc=(ctx.ve if c == 7 else None))
    return last


def linear_fm(p, ctx, XNsrc, T, wbuf, din, dout, dst, dT, mtile_map=None,
              func=None, bias_pack=None, bias_col0=None, add_to_X=None, w_tok=None,
              in_tok=None, banks=None, out_f32=False, n_tiles=None, extra_evac=None):
    nk = din // 128
    nm = dout // 128
    nt = T // 512 if n_tiles is None else n_tiles
    banks = banks or ctx.banks[:4]
    p.wait("tensor", w_tok)
    p.wait("tensor", in_tok)
    bi = 0
    for mi in range(nm):
        for it in range(nt):
            bank = banks[bi % len(banks)]
            bi += 1
            p.wait("tensor", bank.ev)
            for k in range(nk):
                p.ins("tensor", "matmul", bank.ap[:, 0:512],
                      wbuf[:, (k * dout + mi * 128):(k * dout + (mi + 1) * 128)],
                      XNsrc[:, k * T + it * 512: k * T + (it + 1) * 512],
                      start=(k == 0), stop=(k == nk - 1), inc=(None if k < nk - 1 else ctx.te))
            if add_to_X is not None:
                dsl = add_to_X[:, mi * dT + it * 512: mi * dT + (it + 1) * 512]
                tok = evac(p, ctx, bank, dsl, add_to=dsl, out_slice=bank.ap[:, 0:512])
            else:
                dsl = dst[:, mi * dT + it * 512: mi * dT + (it + 1) * 512]
                b = bias_pack[:, bias_col0 + mi: bias_col0 + mi + 1] if bias_pack is not None else None
                if func is not None or b is not None:
                    tok = evac(p, ctx, bank, dsl, func=(func or AF.Identity), bias=b,
                               out_slice=bank.ap[:, 0:512])
                else:
                    tok = evac(p, ctx, bank, dsl, out_slice=bank.ap[:, 0:512])
            if extra_evac is not None:
                extra_evac(p, ctx, bank, mi, it)
    return tok


class Arena:

    def __init__(self, ap, cols, base=0):
        self.ap = ap
        self.cols = cols
        self.base = base
        self.off = base

    def reset(self):
        self.off = self.base

    def a(self, cols_bf16):
        assert self.off + cols_bf16 <= self.cols, (self.off, cols_bf16, self.cols)
        v = self.ap[:, self.off:self.off + cols_bf16]
        self.off += cols_bf16
        return v

    def a32(self, elems_f32):
        v = self.a(2 * elems_f32)
        return v.bitcast(F32)


def stage_reset(p, ctx, ar, mark=0):
    ar.off = mark
    barrier(p, ctx)


def barrier(p, ctx, engs=ENGS, extra=()):
    sems = [ctx.te, ctx.ve, ctx.se, ctx.gp] + list(extra)
    for e in engs:
        for s in sems:
            if s.n > 0:
                p.wait(e, (s, s.n))


def bcast_row(p, ctx, bank, row_f32, dst, ones_f32, n):
    p.wait("tensor", bank.ev)
    p.ins("tensor", "matmul", bank.ap[:, 0:n], ones_f32[0:1, :], row_f32,
          start=True, stop=True, inc=ctx.te)
    return evac(p, ctx, bank, dst, out_slice=bank.ap[:, 0:n])


def emit_l3_attn(p, ctx, mem, W, T=2048):
    ar = mem["arena"]
    ar.reset()
    X, XN = mem["X"], mem["XN"]
    ONES, ONESB = mem["ONES"], mem["ONESB"]
    QT = 128
    nq = T // QT
    KF = ar.a(8 * T)
    VT = ar.a(16 * 1024)
    WA = ar.a(8 * 1024)
    WB = ar.a(8 * 1024)
    QF = ar.a(8 * QT)
    PT = ar.a(16 * QT)
    AT = QF
    NRM = ar.a32(QT)
    NBC = ar.a32(QT)
    barrier(p, ctx)

    in_tok = mem["in_tok"]
    wk_tok = load_weight(p, ctx, WA, W["Wk"], 1024, 1024, sem=ctx.dma[0])
    wv_tok = load_weight(p, ctx, WB, W["Wv"], 1024, 1024, sem=ctx.dma[1])
    kf_tok = linear_fm(p, ctx, XN, T, WA, 1024, 1024, KF, T, w_tok=wk_tok, in_tok=in_tok)
    p.wait("tensor", wv_tok)
    for mi in range(16):
        for half in range(2):
            bank = ctx.banks[(mi * 2 + half) % 4]
            p.wait("tensor", bank.ev)
            for k in range(8):
                p.ins("tensor", "matmul", bank.ap[:, 0:512],
                      XN[:, k * T + mi * 128: k * T + (mi + 1) * 128],
                      WB[:, k * 1024 + half * 512: k * 1024 + (half + 1) * 512],
                      start=(k == 0), stop=(k == 7), inc=(None if k < 7 else ctx.te))
            evac(p, ctx, bank, VT[:, mi * 1024 + half * 512: mi * 1024 + (half + 1) * 512],
                 out_slice=bank.ap[:, 0:512])
    vt_tok = (ctx.ve, ctx.ve.n)
    wq_tok = load_weight(p, ctx, WA, W["Wq"], 1024, 1024, waits=[kf_tok], sem=ctx.dma[0])
    wo_tok = load_weight(p, ctx, WB, W["Wo"], 1024, 1024, waits=[vt_tok], sem=ctx.dma[1])

    for qi in range(nq):
        q0 = qi * QT
        p.wait("tensor", wq_tok)
        for mi in range(8):
            bank = ctx.banks[mi % 2]
            p.wait("tensor", bank.ev)
            for k in range(8):
                p.ins("tensor", "matmul", bank.ap[:, 0:QT],
                      WA[:, k * 1024 + mi * 128: k * 1024 + (mi + 1) * 128],
                      XN[:, k * T + q0: k * T + q0 + QT],
                      start=(k == 0), stop=(k == 7), inc=(None if k < 7 else ctx.te))
            evac(p, ctx, bank, QF[:, mi * QT:(mi + 1) * QT], func=AF.Identity, scale=1.0 / 32.0,
                 out_slice=bank.ap[:, 0:QT])
        qf_tok = (ctx.se, ctx.se.n)
        nkt = qi + 1
        p.wait("tensor", qf_tok)
        for kt in range(nkt):
            bank = ctx.banks[2 + (kt % 2)]
            p.wait("tensor", bank.ev)
            for k in range(8):
                p.ins("tensor", "matmul", bank.ap[:, 0:QT],
                      KF[:, k * T + kt * 128: k * T + (kt + 1) * 128],
                      QF[:, k * QT:(k + 1) * QT],
                      start=(k == 0), stop=(k == 7), inc=(None if k < 7 else ctx.te))
            tok = evac(p, ctx, bank, PT[:, kt * QT:(kt + 1) * QT], func=AF.Exp,
                       bias=mem["SCAL"][:, SCAL_COLS["m50"]:SCAL_COLS["m50"] + 1],
                       out_slice=bank.ap[:, 0:QT])
            if kt == qi:
                p.wait("vector", tok)
                tok = p.ins("vector", "tensor_tensor", PT[:, kt * QT:(kt + 1) * QT],
                            PT[:, kt * QT:(kt + 1) * QT], mem["CM0"][:, 0:QT], op=ALU.mult, inc=ctx.ve)
                bank.ev = tok
        pt_tok = [(ctx.se, ctx.se.n), (ctx.ve, ctx.ve.n)]
        b5 = ctx.banks[4]
        p.wait("tensor", pt_tok)
        p.wait("tensor", b5.ev)
        for kt in range(nkt):
            p.ins("tensor", "matmul", b5.ap[0:1, 0:QT], ONESB[:, 0:1], PT[:, kt * QT:(kt + 1) * QT],
                  start=(kt == 0), stop=(kt == nkt - 1), inc=(None if kt < nkt - 1 else ctx.te))
        p.wait("vector", (ctx.te, ctx.te.n))
        b5.ev = p.ins("vector", "reciprocal", NRM[0:1, 0:QT], b5.ap[0:1, 0:QT], inc=ctx.ve)
        nb_tok = bcast_row(p, ctx, b5, NRM[0:1, 0:QT], NBC, ONES, QT)
        p.wait("vector", nb_tok)
        for vch in range(8):
            bank = ctx.banks[vch % 2]
            p.wait("tensor", bank.ev)
            for kt in range(nkt):
                p.ins("tensor", "matmul", bank.ap[:, 0:QT],
                      VT[:, kt * 1024 + vch * 128: kt * 1024 + (vch + 1) * 128],
                      PT[:, kt * QT:(kt + 1) * QT],
                      start=(kt == 0), stop=(kt == nkt - 1), inc=(None if kt < nkt - 1 else ctx.te))
            p.wait("vector", nb_tok)
            evac(p, ctx, bank, AT[:, vch * QT:(vch + 1) * QT], mult_by=NBC,
                 out_slice=bank.ap[:, 0:QT])
        at_tok = (ctx.ve, ctx.ve.n)
        p.wait("tensor", [wo_tok, at_tok])
        for mi in range(8):
            bank = ctx.banks[2 + (mi % 2)]
            p.wait("tensor", bank.ev)
            for k in range(8):
                p.ins("tensor", "matmul", bank.ap[:, 0:QT],
                      WB[:, k * 1024 + mi * 128: k * 1024 + (mi + 1) * 128],
                      AT[:, k * QT:(k + 1) * QT],
                      start=(k == 0), stop=(k == 7), inc=(None if k < 7 else ctx.te))
            xs = X[:, mi * T + q0: mi * T + q0 + QT]
            evac(p, ctx, bank, xs, add_to=xs, out_slice=bank.ap[:, 0:QT])
    return (ctx.ve, ctx.ve.n)


def xnf32_tile(p, ctx, X, A_bc, B_bc, dst, T, it, w=512, tmp=None):
    for c in range(8):
        xs = X[:, c * T + it * w: c * T + (it + 1) * w]
        p.ins("vector", "tensor_tensor", dst[:, c * w:(c + 1) * w], xs,
              A_bc[:, it * w:(it + 1) * w], op=ALU.mult)
        tok = p.ins("vector", "tensor_tensor", dst[:, c * w:(c + 1) * w],
                    dst[:, c * w:(c + 1) * w], B_bc[:, it * w:(it + 1) * w],
                    op=ALU.subtract, inc=(ctx.ve if c == 7 else None))
    return tok


def sel_bcast(p, ctx, bank, onehot4, rows_f32, dst, T, ncols=None, nb=512):
    ncols = ncols or T
    ntile = (ncols + nb - 1) // nb
    for i in range(ntile):
        n = min(nb, ncols - i * nb)
        p.wait("tensor", bank.ev)
        p.ins("tensor", "matmul", bank.ap[:, 0:n], onehot4, rows_f32[0:4, i * nb:i * nb + n],
              start=True, stop=True, inc=ctx.te)
        evac(p, ctx, bank, dst[:, i * nb:i * nb + n], out_slice=bank.ap[:, 0:n])
    return (ctx.ve, ctx.ve.n)


def emit_gate_argmax4(p, ctx, mem, Wg_dram, bg_dram, OHF, T=2048, mode="argmax"):
    ar = mem["arena"]
    X, A_bc, B_bc = mem["X"], mem["A_bc"], mem["B_bc"]
    ne = 4 if mode == "argmax" else 1
    WG = ar.a32(8 * ne)
    XF = ar.a32(8 * 512)
    GR = ar.a32(64)
    SC = ar.a32(16 * 8)
    wg_tok = p.ins("gpsimd", "dma_start", out=WG.rearrange("p (k m) -> p k m", k=8),
                   in_=Wg_dram.rearrange("(k p) m -> p k m", p=128), inc=ctx.dma[6])
    nb = 4 if mode == "argmax" else 1
    if bg_dram is not None:
        BGR = ar.a32(8)
        bgr_tok = p.ins("sync", "dma_start", out=BGR[0:1, 0:nb],
                        in_=bg_dram.unsqueeze(0), inc=ctx.dma[7])
        b = ctx.banks[5]
        p.wait("tensor", bgr_tok)
        bcast_row(p, ctx, b, BGR[0:1, 0:nb], GR[:, 0:nb], mem["ONES"], nb)
    else:
        zt = p.ins("vector", "memset", GR[:, 0:nb], 0.0, inc=ctx.ve)
        p.wait("vector", zt)
    bank = ctx.banks[4]
    xf_done = {}
    for tc in range(T // 128):
        it4 = tc // 4
        j = tc % 4
        if it4 not in xf_done:
            p.wait("vector", (ctx.te, ctx.te.n))
            xf_tok = xnf32_tile(p, ctx, X, A_bc, B_bc, XF, T, it4, w=512)
            xf_done = {it4: xf_tok}
        xf_tok = xf_done[it4]
        p.wait("tensor", [wg_tok, xf_tok])
        p.wait("tensor", bank.ev)
        for k in range(8):
            p.ins("tensor", "matmul", bank.ap[0:128, 0:ne], XF[:, k * 512 + j * 128: k * 512 + (j + 1) * 128],
                  WG[:, k * ne:(k + 1) * ne], start=(k == 0), stop=(k == 7),
                  inc=(None if k < 7 else ctx.te))
        lg = SC[:, 0:8]
        p.wait("vector", (ctx.te, ctx.te.n))
        if mode == "argmax":
            t1_ = p.ins("vector", "tensor_tensor", lg[:, 0:4], bank.ap[:, 0:4], GR[:, 0:4], op=ALU.add, inc=ctx.ve)
            p.wait("vector", t1_)
            t2_ = p.ins("vector", "tensor_reduce", lg[:, 4:5], lg[:, 0:4], axis=mybir.AxisListType.X, op=ALU.max, inc=ctx.ve)
            p.wait("vector", t2_)
            bank.ev = p.ins("vector", "tensor_scalar", mem["OHTM"][:, tc * 4:(tc + 1) * 4],
                            lg[:, 0:4], lg[:, 4:5], None, op0=ALU.is_equal, inc=ctx.ve)
        else:
            def vf(meth, *a, **k):
                t = p.ins("vector", meth, *a, inc=ctx.ve, **k)
                p.wait("vector", t)
                return t
            vf("tensor_tensor", lg[:, 0:1], bank.ap[:, 0:1], GR[:, 0:1], op=ALU.add)
            vf("tensor_copy", mem["ITMP"][:, 0:1], lg[:, 0:1])
            vf("tensor_copy", lg[:, 1:2], mem["ITMP"][:, 0:1])
            vf("tensor_tensor", lg[:, 2:3], lg[:, 1:2], lg[:, 0:1], op=ALU.is_gt)
            vf("tensor_tensor", lg[:, 1:2], lg[:, 1:2], lg[:, 2:3], op=ALU.subtract)
            vf("tensor_scalar", lg[:, 2:3], lg[:, 1:2], 0.25, None, op0=ALU.mult)
            vf("tensor_copy", mem["ITMP"][:, 0:1], lg[:, 2:3])
            vf("tensor_copy", lg[:, 3:4], mem["ITMP"][:, 0:1])
            vf("tensor_tensor", lg[:, 4:5], lg[:, 3:4], lg[:, 2:3], op=ALU.is_gt)
            vf("tensor_tensor", lg[:, 3:4], lg[:, 3:4], lg[:, 4:5], op=ALU.subtract)
            vf("tensor_scalar", lg[:, 3:4], lg[:, 3:4], -4.0, None, op0=ALU.mult)
            vf("tensor_tensor", lg[:, 1:2], lg[:, 1:2], lg[:, 3:4], op=ALU.add)
            bank.ev = p.ins("vector", "tensor_scalar", mem["OHTM"][:, tc * 4:(tc + 1) * 4],
                            mem["IOTA4"][:, 0:4], lg[:, 1:2], None, op0=ALU.is_equal, inc=ctx.ve)
    tb = ctx.tbanks[0]
    for tc in range(T // 128):
        p.wait("tensor", [tb.ev, (ctx.ve, ctx.ve.n)])
        p.ins("tensor", "transpose", tb.ap[0:4, 0:128], mem["OHTM"][:, tc * 4:(tc + 1) * 4],
              mem["IDENT"], inc=ctx.te)
        tok = evac(p, ctx, tb, OHF[0:4, tc * 128:(tc + 1) * 128], out_slice=tb.ap[0:4, 0:128])
    return tok


def emit_dense_moe(p, ctx, mem, W, rows_mask, T=2048, n_exp=4, dff=4096, act=AF.Gelu,
                   b1_pack=None, b1_col0=0, b2_pack=None, b2_col0=0,
                   dst=None, dst_slice=None, t0=0, Tn=2048, in_src=None, in_tok=None,
                   strip=1024, shared_wslot=False, in_T=None, in_t0=None):
    ar = mem["arena"]
    XN = in_src if in_src is not None else mem["XN"]
    in_T = in_T if in_T is not None else T
    in_t0 = in_t0 if in_t0 is not None else t0
    ns = dff // strip
    nk = strip // 128
    nt = Tn // 512
    W1S = ar.a(8 * strip)
    W2S = W1S if shared_wslot else ar.a((strip // 128) * 1024)
    if shared_wslot:
        assert strip == 1024
    HS = ar.a(nk * Tn)
    OHBC = ar.a(Tn)
    TMPV = ar.a32(512)
    if dst_slice is None:
        X = dst if dst is not None else mem["X"]
        def dst_slice(mi, a, b):
            return X[:, mi * T + t0 + a: mi * T + t0 + b]
    toks = []
    for e in range(n_exp):
        b5 = ctx.banks[5]
        for i in range(nt):
            p.wait("tensor", b5.ev)
            p.ins("tensor", "matmul", b5.ap[:, 0:512], mem["OH4"][0:4, e * 128:(e + 1) * 128],
                  rows_mask[0:4, t0 + i * 512: t0 + (i + 1) * 512], start=True, stop=True, inc=ctx.te)
            evac(p, ctx, b5, OHBC[:, i * 512:(i + 1) * 512], out_slice=b5.ap[:, 0:512])
        ob_tok = (ctx.ve, ctx.ve.n)
        for s in range(ns):
            w1_tok = p.ins("gpsimd", "dma_start", out=W1S.rearrange("p (k m) -> p k m", k=8),
                           in_=W["W1"][e][:, s * strip:(s + 1) * strip].rearrange("(k p) m -> p k m", p=128),
                           waits=[(ctx.te, ctx.te.n)], inc=ctx.dma[4])
            if not shared_wslot:
                w2_tok = p.ins("gpsimd", "dma_start", out=W2S.rearrange("p (k m) -> p k m", k=nk),
                               in_=W["W2"][e][s * strip:(s + 1) * strip, :].rearrange("(k p) m -> p k m", p=128),
                               waits=[(ctx.te, ctx.te.n)], inc=ctx.dma[5])
            p.wait("tensor", [w1_tok, in_tok])
            for mi in range(nk):
                for tq in range(nt):
                    bank = ctx.banks[(mi * nt + tq) % 2]
                    p.wait("tensor", bank.ev)
                    for k in range(8):
                        p.ins("tensor", "matmul", bank.ap[:, 0:512],
                              W1S[:, k * strip + mi * 128: k * strip + (mi + 1) * 128],
                              XN[:, k * in_T + in_t0 + tq * 512: k * in_T + in_t0 + (tq + 1) * 512],
                              start=(k == 0), stop=(k == 7), inc=(None if k < 7 else ctx.te))
                    evac(p, ctx, bank, HS[:, mi * Tn + tq * 512: mi * Tn + (tq + 1) * 512],
                         func=act, bias=(b1_pack[:, b1_col0 + e * (dff // 128) + s * nk + mi:
                                                 b1_col0 + e * (dff // 128) + s * nk + mi + 1]
                                         if b1_pack is not None else None),
                         out_slice=bank.ap[:, 0:512])
            h_tok = (ctx.se, ctx.se.n)
            if shared_wslot:
                w2_tok = p.ins("gpsimd", "dma_start", out=W2S.rearrange("p (k m) -> p k m", k=nk),
                               in_=W["W2"][e][s * strip:(s + 1) * strip, :].rearrange("(k p) m -> p k m", p=128),
                               waits=[(ctx.te, ctx.te.n)], inc=ctx.dma[5])
            p.wait("tensor", [w2_tok, h_tok])
            p.wait("vector", ob_tok)
            last = (s == ns - 1)
            for mi in range(8):
                for tq in range(nt):
                    bank = ctx.banks[2 + (mi * nt + tq) % 2]
                    p.wait("tensor", bank.ev)
                    for k in range(nk):
                        p.ins("tensor", "matmul", bank.ap[:, 0:512],
                              W2S[:, k * 1024 + mi * 128: k * 1024 + (mi + 1) * 128],
                              HS[:, k * Tn + tq * 512: k * Tn + (tq + 1) * 512],
                              start=(k == 0), stop=(k == nk - 1), inc=(None if k < nk - 1 else ctx.te))
                    p.wait("vector", (ctx.te, ctx.te.n))
                    if last and b2_pack is not None:
                        p.ins("vector", "scalar_tensor_tensor", TMPV, bank.ap[:, 0:512],
                              b2_pack[:, b2_col0 + e * 8 + mi: b2_col0 + e * 8 + mi + 1],
                              OHBC[:, tq * 512:(tq + 1) * 512], op0=ALU.add, op1=ALU.mult)
                    else:
                        p.ins("vector", "scalar_tensor_tensor", TMPV, bank.ap[:, 0:512], 1.0,
                              OHBC[:, tq * 512:(tq + 1) * 512], op0=ALU.mult, op1=ALU.mult)
                    ds = dst_slice(mi, tq * 512, (tq + 1) * 512)
                    tok = p.ins("vector", "tensor_tensor", ds, ds, TMPV, op=ALU.add, inc=ctx.ve)
                    bank.ev = tok
            toks.append(tok)
    return toks[-1]


def simrun(nc, in_map, out_names, require_finite=False):
    from concourse.bass_interp import CoreSim, DefaultScheduler
    nc.finalize()
    try:
        nc.detect_race_conditions = False
    except Exception:
        pass
    sim = CoreSim(nc, require_finite=require_finite, require_nnan=False)
    for k, v in in_map.items():
        sim.tensor(k)[:] = v
    sim.simulate(check_with_hw=False)
    return {k: np.array(sim.tensor(k)) for k in out_names}


def vfence(p, ctx, meth, *a, **k):
    t = p.ins("vector", meth, *a, inc=ctx.ve, **k)
    p.wait("vector", t)
    return t


def emit_l1_ffn(p, ctx, mem, W, bias, T=2048):
    ar = mem["arena"]
    X, XN, A_bc, B_bc = mem["X"], mem["XN"], mem["A_bc"], mem["B_bc"]
    ONES, ONESB = mem["ONES"], mem["ONESB"]
    SCAL = mem["SCAL"]
    PK = bias["pack"]
    QT = 256
    nq = T // QT
    x1d = W["x1d"]
    ar.reset()
    ATT = ar.a(8 * T)
    dyn0 = ar.off

    for ps in range(4):
        ar.off = dyn0
        KFq = ar.a(2 * T)
        VTq = ar.a(16 * QT)
        WS = ar.a(8 * 256)
        QF = ar.a(2 * QT)
        PT = ar.a(16 * QT)
        RS = ar.a32(QT)
        NBC = ar.a32(QT)
        TMP = ar.a32(QT)
        if ps == 0:
            barrier(p, ctx)
        c0 = ps * 256
        wk_tok = p.ins("gpsimd", "dma_start", out=WS.rearrange("p (k m) -> p k m", k=8),
                       in_=W["Wk"][:, c0:c0 + 256].rearrange("(k p) m -> p k m", p=128),
                       waits=[(ctx.te, ctx.te.n)], inc=ctx.dma[0])
        p.wait("tensor", [wk_tok, mem["in_tok"]])
        for mi in range(2):
            for it in range(4):
                bank = ctx.banks[(mi * 4 + it) % 2]
                p.wait("tensor", bank.ev)
                for k in range(8):
                    p.ins("tensor", "matmul", bank.ap[:, 0:512],
                          WS[:, k * 256 + mi * 128: k * 256 + (mi + 1) * 128],
                          XN[:, k * T + it * 512: k * T + (it + 1) * 512],
                          start=(k == 0), stop=(k == 7), inc=(None if k < 7 else ctx.te))
                evac(p, ctx, bank, KFq[:, mi * T + it * 512: mi * T + (it + 1) * 512],
                     func=AF.Identity, bias=PK[:, bias["bk"] + 2 * ps + mi: bias["bk"] + 2 * ps + mi + 1],
                     out_slice=bank.ap[:, 0:512])
        kf_tok = (ctx.se, ctx.se.n)
        wv_tok = p.ins("gpsimd", "dma_start", out=WS.rearrange("p (k m) -> p k m", k=8),
                       in_=W["Wv"][:, c0:c0 + 256].rearrange("(k p) m -> p k m", p=128),
                       waits=[kf_tok], inc=ctx.dma[0])
        p.wait("tensor", wv_tok)
        for mi in range(16):
            bank = ctx.banks[mi % 2]
            p.wait("tensor", bank.ev)
            for k in range(8):
                p.ins("tensor", "matmul", bank.ap[:, 0:256],
                      XN[:, k * T + mi * 128: k * T + (mi + 1) * 128],
                      WS[:, k * 256:(k + 1) * 256],
                      start=(k == 0), stop=(k == 7), inc=(None if k < 7 else ctx.te))
            evac(p, ctx, bank, VTq[:, mi * 256:(mi + 1) * 256], out_slice=bank.ap[:, 0:256])
        vt_tok = (ctx.ve, ctx.ve.n)
        wq_tok = p.ins("gpsimd", "dma_start", out=WS.rearrange("p (k m) -> p k m", k=8),
                       in_=W["Wq"][:, c0:c0 + 256].rearrange("(k p) m -> p k m", p=128),
                       waits=[vt_tok], inc=ctx.dma[0])
        p.wait("tensor", wq_tok)
        for qi in range(nq):
            q0 = qi * QT
            for mi in range(2):
                bank = ctx.banks[2 + mi % 2]
                p.wait("tensor", bank.ev)
                for k in range(8):
                    p.ins("tensor", "matmul", bank.ap[:, 0:QT],
                          WS[:, k * 256 + mi * 128: k * 256 + (mi + 1) * 128],
                          XN[:, k * T + q0: k * T + q0 + QT],
                          start=(k == 0), stop=(k == 7), inc=(None if k < 7 else ctx.te))
                evac(p, ctx, bank, QF[:, mi * QT:(mi + 1) * QT],
                     func=AF.Identity, bias=PK[:, bias["bq"] + 2 * ps + mi: bias["bq"] + 2 * ps + mi + 1],
                     out_slice=bank.ap[:, 0:QT])
            qf_tok = (ctx.se, ctx.se.n)
            for hq in range(4):
                rb = 64 * (hq % 2)
                cc = hq // 2
                bsc = ctx.banks[2 + (hq % 2)]
                p.wait("tensor", qf_tok)
                for kt in range(16):
                    p.wait("tensor", bsc.ev)
                    p.ins("tensor", "matmul", bsc.ap[:, 0:QT],
                          KFq[rb:rb + 64, cc * T + kt * 128: cc * T + (kt + 1) * 128],
                          QF[rb:rb + 64, cc * QT:(cc + 1) * QT],
                          start=True, stop=True, inc=ctx.te)
                    evac(p, ctx, bsc, PT[:, kt * QT:(kt + 1) * QT], func=AF.Exp, scale=0.125,
                         bias=SCAL[:, SCAL_COLS["m10"]:SCAL_COLS["m10"] + 1],
                         out_slice=bsc.ap[:, 0:QT])
                pt_tok = (ctx.se, ctx.se.n)
                b5 = ctx.banks[4]
                p.wait("tensor", pt_tok)
                p.wait("tensor", b5.ev)
                for kt in range(16):
                    p.ins("tensor", "matmul", b5.ap[0:1, 0:QT], ONESB[:, 0:1], PT[:, kt * QT:(kt + 1) * QT],
                          start=(kt == 0), stop=(kt == 15), inc=(None if kt < 15 else ctx.te))
                p.wait("vector", (ctx.te, ctx.te.n))
                b5.ev = vfence(p, ctx, "reciprocal", RS[0:1, 0:QT], b5.ap[0:1, 0:QT])
                p.wait("tensor", b5.ev)
                p.ins("tensor", "matmul", b5.ap[0:64, 0:QT], ONES[0:1, 0:64], RS[0:1, 0:QT],
                      start=True, stop=True, inc=ctx.te)
                p.wait("vector", (ctx.te, ctx.te.n))
                b5.ev = vfence(p, ctx, "tensor_copy", NBC[0:64, 0:QT], b5.ap[0:64, 0:QT])
                b6 = ctx.banks[5]
                p.wait("tensor", b6.ev)
                for kt in range(16):
                    p.ins("tensor", "matmul", b6.ap[0:64, 0:QT],
                          VTq[:, kt * 256 + hq * 64: kt * 256 + (hq + 1) * 64],
                          PT[:, kt * QT:(kt + 1) * QT],
                          start=(kt == 0), stop=(kt == 15), inc=(None if kt < 15 else ctx.te))
                p.wait("vector", (ctx.te, ctx.te.n))
                h = 4 * ps + hq
                ac = h // 2
                arow = 64 * (h % 2)
                asl = ATT[arow:arow + 64, ac * T + q0: ac * T + q0 + QT]
                p.ins("vector", "tensor_tensor", TMP[0:64, 0:QT], b6.ap[0:64, 0:QT], NBC[0:64, 0:QT], op=ALU.mult)
                b6.ev = p.ins("vector", "tensor_scalar", asl, TMP[0:64, 0:QT],
                              PK[arow:arow + 64, bias["bv"] + ac: bias["bv"] + ac + 1], None,
                              op0=ALU.add, inc=ctx.ve)
    att_tok = (ctx.ve, ctx.ve.n)

    ar.off = dyn0
    WO = ar.a(8 * 1024)
    S32 = ar.a32(8 * QT)
    SQT = ar.a32(QT)
    SRCB = ar.a(8 * QT)
    X1T = ar.a(8 * QT)
    RWG = ar.a32(8)
    RWB = ar.a32(8)
    CGR = ar.a32(16)
    RROW = ar.a32(4 * QT)
    AB1 = ar.a32(2 * QT)
    barrier(p, ctx)
    wo_tok = load_weight(p, ctx, WO, W["Wo"], 1024, 1024, sem=ctx.dma[1])
    rw_tok = p.ins("sync", "dma_start", out=RWG.rearrange("p (k m) -> p k m", k=8),
                   in_=W["router_w"].rearrange("(k p) m -> p k m", p=128), inc=ctx.dma[2],
                   allow_slow_non_contiguous=True)
    rb_tok = p.ins("sync", "dma_start", out=CGR[0:1, 8:9], in_=W["router_b"].unsqueeze(0), inc=ctx.dma[2])
    p.wait("vector", [rw_tok, rb_tok])
    p.ins("vector", "tensor_tensor", RWB[:, 0:8], RWG[:, 0:8], PK[:, bias["b1ln"]:bias["b1ln"] + 8], op=ALU.mult)
    t = p.ins("vector", "tensor_tensor", RWG[:, 0:8], RWG[:, 0:8], PK[:, bias["g1"]:bias["g1"] + 8], op=ALU.mult, inc=ctx.ve)
    p.wait("vector", t)
    b5 = ctx.banks[4]
    b6 = ctx.banks[5]
    p.wait("tensor", (ctx.ve, ctx.ve.n))
    p.wait("tensor", [b5.ev, b6.ev])
    p.ins("tensor", "matmul", b5.ap[0:1, 0:8], ONES[:, 0:1], RWG[:, 0:8], start=True, stop=True, inc=ctx.te)
    p.ins("tensor", "matmul", b6.ap[0:1, 0:8], ONES[:, 0:1], RWB[:, 0:8], start=True, stop=True, inc=ctx.te)
    p.wait("vector", (ctx.te, ctx.te.n))
    vfence(p, ctx, "tensor_reduce", CGR[0:1, 0:1], b5.ap[0:1, 0:8], axis=mybir.AxisListType.X, op=ALU.add)
    b5.ev = (ctx.ve, ctx.ve.n)
    b6.ev = vfence(p, ctx, "tensor_reduce", CGR[0:1, 1:2], b6.ap[0:1, 0:8], axis=mybir.AxisListType.X, op=ALU.add)
    vfence(p, ctx, "tensor_tensor", CGR[0:1, 1:2], CGR[0:1, 1:2], CGR[0:1, 8:9], op=ALU.add)

    p.wait("tensor", [att_tok, wo_tok])
    bS, bQ, bK = ctx.banks[2], ctx.banks[3], ctx.banks[4]
    for qi in range(nq):
        q0 = qi * QT
        for c in range(8):
            bank = ctx.banks[c % 2]
            p.wait("tensor", bank.ev)
            for k in range(8):
                p.ins("tensor", "matmul", bank.ap[:, 0:QT],
                      WO[:, k * 1024 + c * 128: k * 1024 + (c + 1) * 128],
                      ATT[:, k * T + q0: k * T + q0 + QT],
                      start=(k == 0), stop=(k == 7), inc=(None if k < 7 else ctx.te))
            p.wait("vector", (ctx.te, ctx.te.n))
            xf = S32[:, c * QT:(c + 1) * QT]
            p.ins("vector", "tensor_tensor", xf, X[:, c * T + q0: c * T + q0 + QT],
                  A_bc[:, q0:q0 + QT], op=ALU.mult)
            p.ins("vector", "tensor_tensor", xf, xf, B_bc[:, q0:q0 + QT], op=ALU.subtract)
            bank.ev = p.ins("vector", "scalar_tensor_tensor", xf, bank.ap[:, 0:QT],
                            PK[:, bias["bo"] + c: bias["bo"] + c + 1], xf,
                            op0=ALU.add, op1=ALU.add, inc=ctx.ve)
            p.wait("scalar", bank.ev)
            p.ins("scalar", "activation", SRCB[:, c * QT:(c + 1) * QT], xf, AF.Identity)
            if c > 0:
                p.wait("scalar", (ctx.te, ctx.te.n))
            sq_tok = p.ins("scalar", "activation", SQT[:, 0:QT], xf, AF.Square, inc=ctx.se)
            p.wait("tensor", [bS.ev, bQ.ev, bK.ev] if c == 0 else None)
            p.wait("tensor", bank.ev)
            p.ins("tensor", "matmul", bS.ap[0:1, 0:QT], ONES[:, 0:1], xf,
                  start=(c == 0), stop=(c == 7), inc=(None if c < 7 else ctx.te))
            p.wait("tensor", sq_tok)
            p.ins("tensor", "matmul", bQ.ap[0:1, 0:QT], ONES[:, 0:1], SQT[:, 0:QT],
                  start=(c == 0), stop=(c == 7), inc=ctx.te)
            p.ins("tensor", "matmul", bK.ap[0:1, 0:QT], RWG[:, c:c + 1], xf,
                  start=(c == 0), stop=(c == 7), inc=(None if c < 7 else ctx.te))
        sums_tok = (ctx.te, ctx.te.n)
        m1 = RROW[0:1, 0:QT]
        t1 = RROW[0:1, QT:2 * QT]
        r1 = RROW[0:1, 2 * QT:3 * QT]
        ky = RROW[0:1, 3 * QT:4 * QT]
        p.wait("vector", sums_tok)
        p.ins("vector", "tensor_scalar", m1, bS.ap[0:1, 0:QT], 1.0 / 1024, None, op0=ALU.mult)
        p.ins("vector", "tensor_scalar", t1, bQ.ap[0:1, 0:QT], 1.0 / 1024, None, op0=ALU.mult)
        vfence(p, ctx, "tensor_tensor", r1, m1, m1, op=ALU.mult)
        vfence(p, ctx, "tensor_tensor", t1, t1, r1, op=ALU.subtract)
        bS.ev = bQ.ev = (ctx.ve, ctx.ve.n)
        p.wait("scalar", (ctx.ve, ctx.ve.n))
        st_ = p.ins("scalar", "activation", r1, t1, AF.Sqrt,
                    bias=SCAL[0:1, SCAL_COLS["eps"]:SCAL_COLS["eps"] + 1], inc=ctx.se)
        p.wait("vector", st_)
        vfence(p, ctx, "reciprocal", r1, r1)
        vfence(p, ctx, "tensor_scalar", ky, m1, CGR[0:1, 0:1], None, op0=ALU.mult)
        vfence(p, ctx, "tensor_tensor", ky, bK.ap[0:1, 0:QT], ky, op=ALU.subtract)
        bK.ev = (ctx.ve, ctx.ve.n)
        vfence(p, ctx, "tensor_tensor", ky, ky, r1, op=ALU.mult)
        vfence(p, ctx, "tensor_scalar", ky, ky, 1.0, CGR[0:1, 1:2], op0=ALU.mult, op1=ALU.add)
        f0 = RROW[0:1, QT:2 * QT]
        f1 = m1
        fa = SQT[0:1, 0:QT]
        vfence(p, ctx, "tensor_copy", mem["IT256"][0:1, 0:QT], ky)
        vfence(p, ctx, "tensor_copy", f0, mem["IT256"][0:1, 0:QT])
        vfence(p, ctx, "tensor_tensor", fa, f0, ky, op=ALU.is_gt)
        vfence(p, ctx, "tensor_tensor", f0, f0, fa, op=ALU.subtract)
        vfence(p, ctx, "tensor_scalar", fa, f0, 0.25, None, op0=ALU.mult)
        vfence(p, ctx, "tensor_copy", mem["IT256"][0:1, 0:QT], fa)
        vfence(p, ctx, "tensor_copy", ky, mem["IT256"][0:1, 0:QT])
        vfence(p, ctx, "tensor_tensor", fa, ky, fa, op=ALU.is_gt)
        vfence(p, ctx, "tensor_tensor", ky, ky, fa, op=ALU.subtract)
        vfence(p, ctx, "tensor_scalar", ky, ky, -4.0, None, op0=ALU.mult)
        vfence(p, ctx, "tensor_tensor", f0, f0, ky, op=ALU.add)
        b5b = ctx.banks[5]
        p.wait("tensor", [b5b.ev, (ctx.ve, ctx.ve.n)])
        p.ins("tensor", "matmul", b5b.ap[0:4, 0:QT], ONES[0:1, 0:4], f0, start=True, stop=True, inc=ctx.te)
        p.wait("vector", (ctx.te, ctx.te.n))
        b5b.ev = p.ins("vector", "tensor_scalar", mem["OHF"][0:4, q0:q0 + QT], b5b.ap[0:4, 0:QT],
                       mem["IOTA4C"][0:4, 0:1], None, op0=ALU.is_equal, inc=ctx.ve)
        A1 = AB1[:, 0:QT]
        B1 = AB1[:, QT:2 * QT]
        b4 = ctx.banks[4]
        p.wait("tensor", [b4.ev, (ctx.ve, ctx.ve.n)])
        p.ins("tensor", "matmul", b4.ap[:, 0:QT], ONES[0:1, :], r1, start=True, stop=True, inc=ctx.te)
        evac(p, ctx, b4, A1, out_slice=b4.ap[:, 0:QT])
        vfence(p, ctx, "tensor_tensor", m1, m1, r1, op=ALU.mult)
        p.wait("tensor", [b4.ev, (ctx.ve, ctx.ve.n)])
        p.ins("tensor", "matmul", b4.ap[:, 0:QT], ONES[0:1, :], m1, start=True, stop=True, inc=ctx.te)
        evac(p, ctx, b4, B1, out_slice=b4.ap[:, 0:QT])
        bK.ev = b4.ev
        if qi > 0:
            p.wait("vector", x1w_tok)
        for c in range(8):
            xs = X1T[:, c * QT:(c + 1) * QT]
            tmpc = S32[:, c * QT:(c + 1) * QT]
            p.ins("vector", "tensor_tensor", tmpc, SRCB[:, c * QT:(c + 1) * QT], A1, op=ALU.mult)
            p.ins("vector", "tensor_tensor", tmpc, tmpc, B1, op=ALU.subtract)
            x1_tok = p.ins("vector", "tensor_scalar", xs, tmpc,
                           PK[:, bias["g1"] + c: bias["g1"] + c + 1],
                           PK[:, bias["b1ln"] + c: bias["b1ln"] + c + 1],
                           op0=ALU.mult, op1=ALU.add, inc=(ctx.ve if c == 7 else None))
        x1w_tok = p.ins("gpsimd", "dma_start",
                        out=x1d.rearrange("(c p) t -> p c t", p=128)[:, :, q0:q0 + QT],
                        in_=X1T.rearrange("p (c t) -> p c t", c=8),
                        waits=[(ctx.ve, ctx.ve.n), (ctx.dma[3], ctx.dma[3].n)] if qi == 0 else [(ctx.ve, ctx.ve.n)],
                        inc=ctx.dma[3])
    x1_tok = x1w_tok
    oh_tok = (ctx.ve, ctx.ve.n)

    ar.off = 0
    X1H = ar.a(8 * 1024)
    s2base = ar.off
    for half in range(2):
        t0 = half * 1024
        ar.off = s2base
        S2 = ar.a32(8 * 1024)
        barrier(p, ctx)
        x1r_tok = p.ins("gpsimd", "dma_start", out=X1H.rearrange("p (c t) -> p c t", c=8),
                        in_=x1d.rearrange("(c p) t -> p c t", p=128)[:, :, t0:t0 + 1024],
                        waits=[x1_tok], inc=ctx.dma[3])
        p.wait("vector", x1r_tok)
        for c in range(8):
            for tq in range(2):
                s2tok = p.ins("vector", "tensor_copy", S2[:, c * 1024 + tq * 512: c * 1024 + (tq + 1) * 512],
                              X1H[:, c * 1024 + tq * 512: c * 1024 + (tq + 1) * 512],
                              inc=(ctx.ve if (c == 7 and tq == 1) else None))
        def dsl(mi, a, b, S2=S2):
            return S2[:, mi * 1024 + a: mi * 1024 + b]
        moe_tok = emit_dense_moe(p, ctx, mem, {"W1": W["W1"], "W2": W["W2"]}, mem["OHF"],
                                 T=T, n_exp=4, dff=4096, act=AF.Relu,
                                 b1_pack=bias["b1pack"], b1_col0=bias["b1"],
                                 b2_pack=bias["b2pack"], b2_col0=bias["b2"],
                                 dst_slice=dsl, t0=t0, Tn=1024, in_src=X1H, in_T=1024, in_t0=0,
                                 in_tok=[x1r_tok, oh_tok], strip=512)
        if mem.get("dbg_s2") is not None and half == 0:
            dt_ = p.ins("gpsimd", "dma_start", out=mem["dbg_s2"], in_=S2, waits=[moe_tok], inc=ctx.ldG)
            p.wait("vector", dt_)
        p.wait("tensor", moe_tok)
        p.wait("scalar", moe_tok)
        ar.off = s2base + 2 * 8 * 1024
        ROWS2 = ar.a32(3 * 512)
        SQ2 = ar.a32(512)
        AB2 = ar.a32(2 * 512)
        bS, bQ = ctx.banks[2], ctx.banks[3]
        for it in range(2):
            i0 = it * 512
            p.wait("scalar", moe_tok if it == 0 else None)
            p.wait("tensor", moe_tok if it == 0 else None)
            p.wait("tensor", [bS.ev, bQ.ev])
            for c in range(8):
                p.ins("tensor", "matmul", bS.ap[0:1, 0:512], ONES[:, 0:1], S2[:, c * 1024 + i0: c * 1024 + i0 + 512],
                      start=(c == 0), stop=(c == 7), inc=(None if c < 7 else ctx.te))
            for c in range(8):
                if c > 0:
                    p.wait("scalar", (ctx.te, ctx.te.n))
                sq2tok = p.ins("scalar", "activation", SQ2[:, 0:512], S2[:, c * 1024 + i0: c * 1024 + i0 + 512],
                               AF.Square, inc=ctx.se)
                p.wait("tensor", sq2tok)
                p.ins("tensor", "matmul", bQ.ap[0:1, 0:512], ONES[:, 0:1], SQ2[:, 0:512],
                      start=(c == 0), stop=(c == 7), inc=ctx.te)
            m2 = ROWS2[0:1, 0:512]
            t2 = ROWS2[0:1, 512:1024]
            r2 = ROWS2[0:1, 1024:1536]
            p.wait("vector", (ctx.te, ctx.te.n))
            p.ins("vector", "tensor_scalar", m2, bS.ap[0:1, 0:512], 1.0 / 1024, None, op0=ALU.mult)
            p.ins("vector", "tensor_scalar", t2, bQ.ap[0:1, 0:512], 1.0 / 1024, None, op0=ALU.mult)
            if mem.get("dbg_rows") is not None and half == 0 and it == 0:
                dr_ = p.ins("vector", "tensor_copy", mem["dbg_rows_stage"][0:1, 0:1024], ROWS2[0:1, 0:1024], inc=ctx.ve)
                p.wait("vector", dr_)
                dr2_ = p.ins("gpsimd", "dma_start", out=mem["dbg_rows"], in_=mem["dbg_rows_stage"][0:1, 0:1024],
                             waits=[dr_], inc=ctx.ldG)
            vfence(p, ctx, "tensor_tensor", r2, m2, m2, op=ALU.mult)
            vfence(p, ctx, "tensor_tensor", t2, t2, r2, op=ALU.subtract)
            bS.ev = bQ.ev = (ctx.ve, ctx.ve.n)
            p.wait("scalar", (ctx.ve, ctx.ve.n))
            st2_ = p.ins("scalar", "activation", r2, t2, AF.Sqrt,
                         bias=SCAL[0:1, SCAL_COLS["eps"]:SCAL_COLS["eps"] + 1], inc=ctx.se)
            p.wait("vector", st2_)
            vfence(p, ctx, "reciprocal", r2, r2)
            b4 = ctx.banks[4]
            p.wait("tensor", [b4.ev, (ctx.ve, ctx.ve.n)])
            p.ins("tensor", "matmul", b4.ap[:, 0:512], ONES[0:1, :], r2, start=True, stop=True, inc=ctx.te)
            evac(p, ctx, b4, AB2[:, 0:512], out_slice=b4.ap[:, 0:512])
            vfence(p, ctx, "tensor_tensor", m2, m2, r2, op=ALU.mult)
            p.wait("tensor", [b4.ev, (ctx.ve, ctx.ve.n)])
            p.ins("tensor", "matmul", b4.ap[:, 0:512], ONES[0:1, :], m2, start=True, stop=True, inc=ctx.te)
            b4e = evac(p, ctx, b4, AB2[:, 512:1024], out_slice=b4.ap[:, 0:512])
            if mem.get("dbg_ab2") is not None and half == 0 and it == 0:
                da_ = p.ins("gpsimd", "dma_start", out=mem["dbg_ab2"], in_=AB2, waits=[b4e], inc=ctx.ldG)
                p.wait("vector", da_)
            for c in range(8):
                s2sl = S2[:, c * 1024 + i0: c * 1024 + i0 + 512]
                p.ins("vector", "tensor_tensor", s2sl, s2sl, AB2[:, 0:512], op=ALU.mult)
                p.ins("vector", "tensor_tensor", s2sl, s2sl, AB2[:, 512:1024], op=ALU.subtract)
                p.ins("vector", "tensor_scalar", s2sl, s2sl,
                      PK[:, bias["g2"] + c: bias["g2"] + c + 1],
                      PK[:, bias["b2ln"] + c: bias["b2ln"] + c + 1], op0=ALU.mult, op1=ALU.add)
                xsl = X[:, c * T + t0 + i0: c * T + t0 + i0 + 512]
                fin = p.ins("vector", "tensor_tensor", xsl, xsl, s2sl, op=ALU.add,
                            inc=(ctx.ve if c == 7 else None))
    return (ctx.ve, ctx.ve.n)


def emit_l2_attn(p, ctx, mem, W, T=2048):
    ar = mem["arena"]
    X, XN = mem["X"], mem["XN"]
    ONES, ONESB, SCAL = mem["ONES"], mem["ONESB"], mem["SCAL"]
    QT = 256
    nq = T // QT
    ar.reset()
    ATT = ar.a(8 * T)
    dyn0 = ar.off
    for ps in range(4):
        ar.off = dyn0
        KFq = ar.a(2 * T)
        VTq = ar.a(16 * QT)
        WS = ar.a(8 * 256)
        WS2 = ar.a(8 * 256)
        QF = ar.a(2 * QT)
        PT = ar.a(16 * QT)
        RS = ar.a32(QT)
        NBC = ar.a32(QT)
        TMP = ar.a32(QT)
        SREL = ar.a32(QT)
        if ps == 0:
            barrier(p, ctx)
        c0 = ps * 256
        wk_tok = p.ins("gpsimd", "dma_start", out=WS.rearrange("p (k m) -> p k m", k=8),
                       in_=W["W_attn"][:, 1024 + c0:1024 + c0 + 256].rearrange("(k p) m -> p k m", p=128),
                       waits=[(ctx.te, ctx.te.n)], inc=ctx.dma[0])
        wv_tok = p.ins("gpsimd", "dma_start", out=WS2.rearrange("p (k m) -> p k m", k=8),
                       in_=W["VK"][:, c0:c0 + 256].rearrange("(k p) m -> p k m", p=128),
                       waits=[(ctx.te, ctx.te.n)], inc=ctx.dma[1])
        p.wait("tensor", [wk_tok, mem["in_tok"]])
        for mi in range(2):
            for it in range(4):
                bank = ctx.banks[(mi * 4 + it) % 2]
                p.wait("tensor", bank.ev)
                for k in range(8):
                    p.ins("tensor", "matmul", bank.ap[:, 0:512],
                          WS[:, k * 256 + mi * 128: k * 256 + (mi + 1) * 128],
                          XN[:, k * T + it * 512: k * T + (it + 1) * 512],
                          start=(k == 0), stop=(k == 7), inc=(None if k < 7 else ctx.te))
                evac(p, ctx, bank, KFq[:, mi * T + it * 512: mi * T + (it + 1) * 512],
                     out_slice=bank.ap[:, 0:512])
        p.wait("tensor", wv_tok)
        for mi in range(16):
            bank = ctx.banks[mi % 2]
            p.wait("tensor", bank.ev)
            for k in range(8):
                p.ins("tensor", "matmul", bank.ap[:, 0:256],
                      XN[:, k * T + mi * 128: k * T + (mi + 1) * 128],
                      WS2[:, k * 256:(k + 1) * 256],
                      start=(k == 0), stop=(k == 7), inc=(None if k < 7 else ctx.te))
            evac(p, ctx, bank, VTq[:, mi * 256:(mi + 1) * 256], out_slice=bank.ap[:, 0:256])
        vt_tok = (ctx.ve, ctx.ve.n)
        wq_tok = p.ins("gpsimd", "dma_start", out=WS.rearrange("p (k m) -> p k m", k=8),
                       in_=W["W_attn"][:, c0:c0 + 256].rearrange("(k p) m -> p k m", p=128),
                       waits=[vt_tok], inc=ctx.dma[0])
        p.wait("tensor", wq_tok)
        for qi in range(nq):
            q0 = qi * QT
            for mi in range(2):
                bank = ctx.banks[2 + mi % 2]
                p.wait("tensor", bank.ev)
                for k in range(8):
                    p.ins("tensor", "matmul", bank.ap[:, 0:QT],
                          WS[:, k * 256 + mi * 128: k * 256 + (mi + 1) * 128],
                          XN[:, k * T + q0: k * T + q0 + QT],
                          start=(k == 0), stop=(k == 7), inc=(None if k < 7 else ctx.te))
                evac(p, ctx, bank, QF[:, mi * QT:(mi + 1) * QT], out_slice=bank.ap[:, 0:QT])
            qf_tok = (ctx.se, ctx.se.n)
            nkt = 2 * (qi + 1)
            for hq in range(4):
                h = 4 * ps + hq
                rb = 64 * (hq % 2)
                cc = hq // 2
                bsc = ctx.banks[2 + (hq % 2)]
                p.wait("tensor", qf_tok)
                etok = None
                for kt in range(nkt):
                    p.wait("tensor", bsc.ev)
                    p.ins("tensor", "matmul", bsc.ap[:, 0:QT],
                          KFq[rb:rb + 64, cc * T + kt * 128: cc * T + (kt + 1) * 128],
                          QF[rb:rb + 64, cc * QT:(cc + 1) * QT],
                          start=True, stop=True, inc=ctx.te)
                    j = (q0 - kt * 128) // 128
                    p.wait("vector", (ctx.te, ctx.te.n))
                    p.wait("vector", etok)
                    if j == 0:
                        pat, jcol = mem["P0M0"], 1
                    elif j == -1:
                        pat, jcol = mem["P0M1"], 1
                    else:
                        pat, jcol = mem["P0"], j + 1
                    p.ins("vector", "tensor_scalar", SREL, pat, mem["ASLP"][:, h:h + 1],
                          mem["AOFF"][:, h * 17 + jcol: h * 17 + jcol + 1], op0=ALU.mult, op1=ALU.add)
                    bsc.ev = p.ins("vector", "scalar_tensor_tensor", TMP, bsc.ap[:, 0:QT], 0.125, SREL,
                                   op0=ALU.mult, op1=ALU.add, inc=ctx.ve)
                    p.wait("scalar", bsc.ev)
                    etok = p.ins("scalar", "activation", PT[:, kt * QT:(kt + 1) * QT], TMP, AF.Exp,
                                 bias=SCAL[:, SCAL_COLS["m10"]:SCAL_COLS["m10"] + 1], inc=ctx.se)
                pt_tok = [(ctx.se, ctx.se.n), (ctx.ve, ctx.ve.n)]
                b5 = ctx.banks[4]
                p.wait("tensor", pt_tok)
                p.wait("tensor", b5.ev)
                for kt in range(nkt):
                    p.ins("tensor", "matmul", b5.ap[0:1, 0:QT], ONESB[:, 0:1], PT[:, kt * QT:(kt + 1) * QT],
                          start=(kt == 0), stop=(kt == nkt - 1), inc=(None if kt < nkt - 1 else ctx.te))
                p.wait("vector", (ctx.te, ctx.te.n))
                b5.ev = vfence(p, ctx, "reciprocal", RS[0:1, 0:QT], b5.ap[0:1, 0:QT])
                p.wait("tensor", b5.ev)
                p.ins("tensor", "matmul", b5.ap[0:64, 0:QT], ONES[0:1, 0:64], RS[0:1, 0:QT],
                      start=True, stop=True, inc=ctx.te)
                p.wait("vector", (ctx.te, ctx.te.n))
                b5.ev = vfence(p, ctx, "tensor_copy", NBC[0:64, 0:QT], b5.ap[0:64, 0:QT])
                b6 = ctx.banks[5]
                p.wait("tensor", b6.ev)
                for kt in range(nkt):
                    p.ins("tensor", "matmul", b6.ap[0:64, 0:QT],
                          VTq[:, kt * 256 + hq * 64: kt * 256 + (hq + 1) * 64],
                          PT[:, kt * QT:(kt + 1) * QT],
                          start=(kt == 0), stop=(kt == nkt - 1), inc=(None if kt < nkt - 1 else ctx.te))
                p.wait("vector", (ctx.te, ctx.te.n))
                ac = h // 2
                arow = 64 * (h % 2)
                asl = ATT[arow:arow + 64, ac * T + q0: ac * T + q0 + QT]
                b6.ev = p.ins("vector", "tensor_tensor", asl, b6.ap[0:64, 0:QT], NBC[0:64, 0:QT],
                              op=ALU.mult, inc=ctx.ve)
    att_tok = (ctx.ve, ctx.ve.n)
    ar.off = dyn0
    WO = ar.a(8 * 1024)
    barrier(p, ctx, engs=("gpsimd",))
    wo_tok = load_weight(p, ctx, WO, W["PKron"], 1024, 1024, sem=ctx.dma[1])
    p.wait("tensor", [att_tok, wo_tok])
    for qi in range(nq):
        q0 = qi * QT
        for c in range(8):
            bank = ctx.banks[c % 2]
            p.wait("tensor", bank.ev)
            for k in range(8):
                p.ins("tensor", "matmul", bank.ap[:, 0:QT],
                      WO[:, k * 1024 + c * 128: k * 1024 + (c + 1) * 128],
                      ATT[:, k * T + q0: k * T + q0 + QT],
                      start=(k == 0), stop=(k == 7), inc=(None if k < 7 else ctx.te))
            xs = X[:, c * T + q0: c * T + q0 + QT]
            evac(p, ctx, bank, xs, add_to=xs, out_slice=bank.ap[:, 0:QT])
    return (ctx.ve, ctx.ve.n)


def emit_l2_ffn(p, ctx, mem, W, T=2048):
    ar = mem["arena"]
    X, XN, A_bc, B_bc = mem["X"], mem["XN"], mem["A_bc"], mem["B_bc"]
    ONES, SCAL = mem["ONES"], mem["SCAL"]
    ar.reset()
    CMBF = ar.a(T)
    dyn0 = ar.off
    XF = ar.a32(8 * 512)
    GWT = ar.a32(8 * 32)
    SC0 = ar.a32(32)
    SCB = ar.a32(32)
    EQ9 = ar.a32(32)
    WSEL = ar.a32(32)
    GRS = ar.a32(64)
    CMB = ar.a(32)
    GBB = ar.a32(32)
    barrier(p, ctx)
    gw_tok = p.ins("sync", "dma_start", out=GWT.rearrange("p (k m) -> p k m", k=8),
                   in_=W["gwT"].rearrange("(k p) m -> p k m", p=128), inc=ctx.dma[2])
    gb_tok = p.ins("sync", "dma_start", out=GBB[0:1, 0:32], in_=W["gate_bias"].unsqueeze(0), inc=ctx.dma[2])
    b5 = ctx.banks[4]
    p.wait("tensor", gb_tok)
    bcast_row(p, ctx, b5, GBB[0:1, 0:32], GBB[:, 0:32], ONES, 32)
    tb = ctx.tbanks[0]
    for tc in range(T // 128):
        it4 = tc // 4
        j = tc % 4
        if j == 0:
            p.wait("vector", (ctx.te, ctx.te.n))
            xf_tok = xnf32_tile(p, ctx, X, A_bc, B_bc, XF, T, it4, w=512)
        bank = ctx.banks[5]
        p.wait("tensor", [gw_tok, xf_tok, bank.ev])
        for k in range(8):
            p.ins("tensor", "matmul", bank.ap[0:128, 0:32],
                  XF[:, k * 512 + j * 128: k * 512 + (j + 1) * 128],
                  GWT[:, k * 32:(k + 1) * 32], start=(k == 0), stop=(k == 7),
                  inc=(None if k < 7 else ctx.te))
        p.wait("scalar", (ctx.te, ctx.te.n))
        sg_tok = p.ins("scalar", "activation", SC0[:, 0:32], bank.ap[0:128, 0:32], AF.Sigmoid, inc=ctx.se)
        bank.ev = sg_tok
        p.wait("vector", sg_tok)
        m1g, m2g, gs = GRS[:, 0:8], GRS[:, 8:16], GRS[:, 16:24]
        t8, g4, th8, ws = GRS[:, 24:32], GRS[:, 32:33], GRS[:, 33:34], GRS[:, 34:35]
        vfence(p, ctx, "tensor_tensor", SCB[:, 0:32], SC0[:, 0:32], GBB[:, 0:32], op=ALU.add)
        sc3 = SCB[:, 0:32].rearrange("p (g f) -> p g f", g=8)
        vfence(p, ctx, "tensor_reduce", m1g, sc3, axis=mybir.AxisListType.X, op=ALU.max)
        m1b = m1g.rearrange("p (g o) -> p g o", o=1).to_broadcast([128, 8, 4])
        vfence(p, ctx, "tensor_tensor", EQ9[:, 0:32].rearrange("p (g f) -> p g f", g=8), sc3, m1b, op=ALU.is_equal)
        vfence(p, ctx, "scalar_tensor_tensor", EQ9[:, 0:32], EQ9[:, 0:32], -1e9, SCB[:, 0:32],
               op0=ALU.mult, op1=ALU.add)
        vfence(p, ctx, "tensor_reduce", m2g, EQ9[:, 0:32].rearrange("p (g f) -> p g f", g=8),
               axis=mybir.AxisListType.X, op=ALU.max)
        vfence(p, ctx, "tensor_tensor", gs, m1g, m2g, op=ALU.add)
        vfence(p, ctx, "max", t8, gs)
        vfence(p, ctx, "tensor_scalar", EQ9[:, 0:8], gs, t8[:, 3:4], None, op0=ALU.is_ge)
        gmb = EQ9[:, 0:8].rearrange("p (g o) -> p g o", o=1).to_broadcast([128, 8, 4])
        vfence(p, ctx, "tensor_tensor", WSEL[:, 0:32].rearrange("p (g f) -> p g f", g=8), sc3, gmb, op=ALU.mult)
        vfence(p, ctx, "max", t8, WSEL[:, 0:32])
        vfence(p, ctx, "tensor_scalar", WSEL[:, 0:32], WSEL[:, 0:32], t8[:, 7:8], None, op0=ALU.is_ge)
        vfence(p, ctx, "tensor_tensor", WSEL[:, 0:32], WSEL[:, 0:32], SC0[:, 0:32], op=ALU.mult)
        vfence(p, ctx, "tensor_reduce", ws, WSEL[:, 0:32], axis=mybir.AxisListType.X, op=ALU.add)
        vfence(p, ctx, "reciprocal", ws, ws)
        vfence(p, ctx, "tensor_scalar", ws, ws, 2.5, None, op0=ALU.mult)
        vfence(p, ctx, "tensor_scalar", CMB[:, 0:32], WSEL[:, 0:32], ws, None, op0=ALU.mult)
        p.wait("tensor", [tb.ev, (ctx.ve, ctx.ve.n)])
        p.ins("tensor", "transpose", tb.ap.bitcast(BF16)[0:32, 0:128], CMB[:, 0:32], mem["IDENT"], inc=ctx.te)
        p.wait("vector", (ctx.te, ctx.te.n))
        tb.ev = p.ins("vector", "tensor_copy", CMBF.bitcast(BF16)[0:32, tc * 128:(tc + 1) * 128] if False else CMBF[0:32, tc * 128:(tc + 1) * 128],
                      tb.ap.bitcast(BF16)[0:32, 0:128], inc=ctx.ve)
    rt_tok = (ctx.ve, ctx.ve.n)
    ar.off = dyn0
    WG = ar.a(8 * 512)
    WU = ar.a(8 * 512)
    WD = ar.a(4 * 1024)
    G = ar.a(4 * T)
    GU = ar.a(4 * T)
    CBC = ar.a(T)
    barrier(p, ctx)
    in_tok = mem["in_tok"]
    for e in range(34):
        shared = (e >= 32)
        nI = 4
        if shared:
            sh = e - 32
            wgd = W["shG"][:, sh * 512:(sh + 1) * 512]
            wud = W["shU"][:, sh * 512:(sh + 1) * 512]
            wdd = W["shD"][sh * 512:(sh + 1) * 512, :]
        else:
            wgd, wud, wdd = W["wg"][e], W["wu"][e], W["wd"][e]
        Iw = 128 * nI
        wg_tok = p.ins("gpsimd", "dma_start", out=WG.rearrange("p (k m) -> p k m", k=8),
                       in_=wgd.rearrange("(k p) m -> p k m", p=128),
                       waits=[(ctx.te, ctx.te.n)], inc=ctx.dma[0])
        wu_tok = p.ins("gpsimd", "dma_start", out=WU.rearrange("p (k m) -> p k m", k=8),
                       in_=wud.rearrange("(k p) m -> p k m", p=128),
                       waits=[(ctx.te, ctx.te.n)], inc=ctx.dma[1])
        wd_tok = p.ins("gpsimd", "dma_start", out=WD.rearrange("p (k m) -> p k m", k=nI),
                       in_=wdd.rearrange("(k p) m -> p k m", p=128),
                       waits=[(ctx.te, ctx.te.n)], inc=ctx.dma[4])
        if not shared:
            b5 = ctx.banks[4]
            for i in range(4):
                p.wait("tensor", [b5.ev, rt_tok])
                p.ins("tensor", "matmul", b5.ap[:, 0:512], mem["OH32"][0:32, e * 128:(e + 1) * 128],
                      CMBF[0:32, i * 512:(i + 1) * 512], start=True, stop=True, inc=ctx.te)
                evac(p, ctx, b5, CBC[:, i * 512:(i + 1) * 512], out_slice=b5.ap[:, 0:512])
            cb_tok = (ctx.ve, ctx.ve.n)
        p.wait("tensor", [wg_tok, in_tok])
        for mi in range(nI):
            for it in range(4):
                bank = ctx.banks[(mi * 4 + it) % 2]
                p.wait("tensor", bank.ev)
                for k in range(8):
                    p.ins("tensor", "matmul", bank.ap[:, 0:512],
                          WG[:, k * Iw + mi * 128: k * Iw + (mi + 1) * 128],
                          XN[:, k * T + it * 512: k * T + (it + 1) * 512],
                          start=(k == 0), stop=(k == 7), inc=(None if k < 7 else ctx.te))
                evac(p, ctx, bank, G[:, mi * T + it * 512: mi * T + (it + 1) * 512],
                     func=AF.Silu, out_slice=bank.ap[:, 0:512])
        g_tok = (ctx.se, ctx.se.n)
        p.wait("tensor", [wu_tok, g_tok])
        for mi in range(nI):
            for it in range(4):
                bank = ctx.banks[2 + (mi * 4 + it) % 2]
                p.wait("tensor", bank.ev)
                for k in range(8):
                    p.ins("tensor", "matmul", bank.ap[:, 0:512],
                          WU[:, k * Iw + mi * 128: k * Iw + (mi + 1) * 128],
                          XN[:, k * T + it * 512: k * T + (it + 1) * 512],
                          start=(k == 0), stop=(k == 7), inc=(None if k < 7 else ctx.te))
                p.wait("vector", (ctx.te, ctx.te.n))
                if not shared:
                    p.wait("vector", cb_tok)
                sl = slice(mi * T + it * 512, mi * T + (it + 1) * 512)
                p.ins("vector", "tensor_tensor", GU[:, sl], bank.ap[:, 0:512], G[:, sl], op=ALU.mult)
                if shared:
                    bank.ev = p.ins("vector", "tensor_copy", GU[:, sl], GU[:, sl], inc=ctx.ve)
                else:
                    bank.ev = p.ins("vector", "tensor_tensor", GU[:, sl], GU[:, sl],
                                    CBC[:, it * 512:(it + 1) * 512], op=ALU.mult, inc=ctx.ve)
        gu_tok = (ctx.ve, ctx.ve.n)
        p.wait("tensor", [wd_tok, gu_tok])
        for mi in range(8):
            for it in range(4):
                bank = ctx.banks[(mi * 4 + it) % 2]
                p.wait("tensor", bank.ev)
                for k in range(nI):
                    p.ins("tensor", "matmul", bank.ap[:, 0:512],
                          WD[:, k * 1024 + mi * 128: k * 1024 + (mi + 1) * 128],
                          GU[:, k * T + it * 512: k * T + (it + 1) * 512],
                          start=(k == 0), stop=(k == nI - 1), inc=(None if k < nI - 1 else ctx.te))
                xs = X[:, mi * T + it * 512: mi * T + (it + 1) * 512]
                evac(p, ctx, bank, xs, add_to=xs, out_slice=bank.ap[:, 0:512])
    return (ctx.ve, ctx.ve.n)


def emit_l0_attn(p, ctx, mem, W, T=2048, causal=False):
    ar = mem["arena"]
    X, XN = mem["X"], mem["XN"]
    ONES, ONESB, SCAL = mem["ONES"], mem["ONESB"], mem["SCAL"]
    PK = mem["PACK"]
    ar.reset()
    ATT = ar.a(8 * T)
    dyn0 = ar.off
    for ps in range(4):
        ar.off = dyn0
        KFq = ar.a(2 * T)
        VTq = ar.a(16 * 256)
        QFq = ar.a(2 * T)
        WS = ar.a(8 * 256)
        PT = ar.a(128)
        RS = ar.a32(64)
        NBC = ar.a32(64)
        if ps == 0:
            barrier(p, ctx)
        c0 = ps * 256
        for which, dst, col0 in [("q", QFq, c0), ("k", KFq, 1024 + c0)]:
            w_tok = p.ins("gpsimd", "dma_start", out=WS.rearrange("p (k m) -> p k m", k=8),
                          in_=W["WQKV"][:, col0:col0 + 256].rearrange("(k p) m -> p k m", p=128),
                          waits=[(ctx.te, ctx.te.n)], inc=ctx.dma[0])
            p.wait("tensor", [w_tok, mem["in_tok"]])
            for mi in range(2):
                for it in range(4):
                    bank = ctx.banks[(mi * 4 + it) % 2]
                    p.wait("tensor", bank.ev)
                    for k in range(8):
                        p.ins("tensor", "matmul", bank.ap[:, 0:512],
                              WS[:, k * 256 + mi * 128: k * 256 + (mi + 1) * 128],
                              XN[:, k * T + it * 512: k * T + (it + 1) * 512],
                              start=(k == 0), stop=(k == 7), inc=(None if k < 7 else ctx.te))
                    if which == "q":
                        evac(p, ctx, bank, dst[:, mi * T + it * 512: mi * T + (it + 1) * 512],
                             func=AF.Identity, scale=0.125, out_slice=bank.ap[:, 0:512])
                    else:
                        evac(p, ctx, bank, dst[:, mi * T + it * 512: mi * T + (it + 1) * 512],
                             out_slice=bank.ap[:, 0:512])
        wv_tok = p.ins("gpsimd", "dma_start", out=WS.rearrange("p (k m) -> p k m", k=8),
                       in_=W["WQKV"][:, 2048 + c0:2048 + c0 + 256].rearrange("(k p) m -> p k m", p=128),
                       waits=[(ctx.te, ctx.te.n)], inc=ctx.dma[0])
        p.wait("tensor", wv_tok)
        for mi in range(16):
            bank = ctx.banks[mi % 2]
            p.wait("tensor", bank.ev)
            for k in range(8):
                p.ins("tensor", "matmul", bank.ap[:, 0:256],
                      XN[:, k * T + mi * 128: k * T + (mi + 1) * 128],
                      WS[:, k * 256:(k + 1) * 256],
                      start=(k == 0), stop=(k == 7), inc=(None if k < 7 else ctx.te))
            evac(p, ctx, bank, VTq[:, mi * 256:(mi + 1) * 256], out_slice=bank.ap[:, 0:256])
        vt_tok = (ctx.ve, ctx.ve.n)
        p.wait("tensor", vt_tok)
        for hq in range(4):
            h = 4 * ps + hq
            rb = 64 * (hq % 2)
            cc = hq // 2
            ac = h // 2
            arow = 64 * (h % 2)
            for c in range(T // 64):
                t0 = c * 64
                pb = 64 * (c % 2)
                bsc = ctx.banks[2 + c % 2]
                bpv = ctx.banks[4 + c % 2]
                p.wait("tensor", bsc.ev)
                p.ins("tensor", "matmul", bsc.ap[0:64, 0:64],
                      KFq[rb:rb + 64, cc * T + t0: cc * T + t0 + 64],
                      QFq[rb:rb + 64, cc * T + t0: cc * T + t0 + 64],
                      start=True, stop=True, inc=ctx.te,
                      tile_position=(rb, 0))
                etok = evac(p, ctx, bsc, PT[pb:pb + 64, 0:64], func=AF.Exp,
                            bias=SCAL[0:64, SCAL_COLS["m10"]:SCAL_COLS["m10"] + 1],
                            out_slice=bsc.ap[0:64, 0:64])
                if causal:
                    p.wait("vector", etok)
                    etok = p.ins("vector", "tensor_tensor", PT[pb:pb + 64, 0:64], PT[pb:pb + 64, 0:64],
                                 mem["CMP64"][pb:pb + 64, 0:64], op=ALU.mult, inc=ctx.ve)
                    bsc.ev = etok
                p.wait("tensor", etok)
                p.ins("tensor", "matmul", bsc.ap[0:1, 64:128], ONESB[pb:pb + 64, 0:1], PT[pb:pb + 64, 0:64],
                      start=True, stop=True, inc=ctx.te, tile_position=(pb, 0))
                p.wait("vector", (ctx.te, ctx.te.n))
                vfence(p, ctx, "reciprocal", RS[0:1, 0:64], bsc.ap[0:1, 64:128])
                p.wait("tensor", (ctx.ve, ctx.ve.n))
                p.ins("tensor", "matmul", bsc.ap[0:64, 64:128], ONES[0:1, 0:64], RS[0:1, 0:64],
                      start=True, stop=True, inc=ctx.te)
                p.wait("vector", (ctx.te, ctx.te.n))
                bsc.ev = vfence(p, ctx, "tensor_copy", NBC[0:64, 0:64], bsc.ap[0:64, 64:128])
                p.wait("tensor", bpv.ev)
                p.ins("tensor", "matmul", bpv.ap[0:64, 0:64],
                      VTq[pb:pb + 64, (t0 // 128) * 256 + hq * 64: (t0 // 128) * 256 + (hq + 1) * 64],
                      PT[pb:pb + 64, 0:64],
                      start=True, stop=True, inc=ctx.te, tile_position=(pb, 0))
                p.wait("vector", [(ctx.te, ctx.te.n), bsc.ev])
                asl = ATT[arow:arow + 64, ac * T + t0: ac * T + t0 + 64]
                bpv.ev = p.ins("vector", "tensor_tensor", asl, bpv.ap[0:64, 0:64],
                               NBC[0:64, 0:64], op=ALU.mult, inc=ctx.ve)
    att_tok = (ctx.ve, ctx.ve.n)
    ar.off = dyn0
    WO = ar.a(8 * 1024)
    TMP0 = ar.a32(512)
    barrier(p, ctx, engs=("gpsimd",))
    wo_tok = load_weight(p, ctx, WO, W["WOT"], 1024, 1024, sem=ctx.dma[1])
    p.wait("tensor", [att_tok, wo_tok])
    for qi in range(4):
        q0 = qi * 512
        for cch in range(8):
            bank = ctx.banks[cch % 2]
            p.wait("tensor", bank.ev)
            for k in range(8):
                p.ins("tensor", "matmul", bank.ap[:, 0:512],
                      WO[:, k * 1024 + cch * 128: k * 1024 + (cch + 1) * 128],
                      ATT[:, k * T + q0: k * T + q0 + 512],
                      start=(k == 0), stop=(k == 7), inc=(None if k < 7 else ctx.te))
            xs = X[:, cch * T + q0: cch * T + q0 + 512]
            p.wait("vector", (ctx.te, ctx.te.n))
            p.ins("vector", "tensor_scalar", TMP0, bank.ap[:, 0:512],
                  PK[:, mem["bout_col"] + cch: mem["bout_col"] + cch + 1], None, op0=ALU.add)
            bank.ev = p.ins("vector", "tensor_tensor", xs, xs, TMP0, op=ALU.add, inc=ctx.ve)
    return (ctx.ve, ctx.ve.n)


def emit_l1_attn(p, ctx, mem, W, bias, T=2048):
    ar = mem["arena"]
    X, XN = mem["X"], mem["XN"]
    ONES, ONESB, SCAL = mem["ONES"], mem["ONESB"], mem["SCAL"]
    PKH = mem["PKH"]
    ar.reset()
    G1 = ar.a(2 * T)
    BVR = ar.a32(1024)
    BM = ar.a(T)
    dyn0 = ar.off
    iot0 = ar.off
    IOTI = ar.a32(T).bitcast(I32)
    barrier(p, ctx)
    it_tok = p.ins("gpsimd", "iota", IOTI, pattern=[[1, T]], base=0, channel_multiplier=0, inc=ctx.gp)
    p.wait("vector", it_tok)
    vfence(p, ctx, "tensor_scalar", IOTI, IOTI, 31, None, op0=ALU.bitwise_and)
    SCR = IOTI.bitcast(F32)
    vfence(p, ctx, "tensor_copy", SCR, IOTI)
    bm_tok = vfence(p, ctx, "tensor_scalar", BM, SCR, 0.0, None, op0=ALU.is_equal)
    bv_tok = p.ins("sync", "dma_start", out=BVR[0:1, 0:1024], in_=W["bv"].unsqueeze(0), inc=ctx.dma[2])
    WG1 = ar.a(8 * 16)
    wg1_tok = p.ins("gpsimd", "dma_start", out=WG1.rearrange("p (k m) -> p k m", k=8),
                    in_=W["Wg1"].rearrange("(k p) m -> p k m", p=128),
                    waits=[(ctx.te, ctx.te.n)], inc=ctx.dma[0])
    p.wait("tensor", [wg1_tok, mem["in_tok"]])
    for it in range(4):
        bank = ctx.banks[it % 2]
        p.wait("tensor", bank.ev)
        for k in range(8):
            p.ins("tensor", "matmul", bank.ap[0:16, 0:512], WG1[:, k * 16:(k + 1) * 16],
                  XN[:, k * T + it * 512: k * T + (it + 1) * 512],
                  start=(k == 0), stop=(k == 7), inc=(None if k < 7 else ctx.te))
        evac(p, ctx, bank, G1[0:16, it * 512:(it + 1) * 512], func=AF.Silu,
             bias=mem["PACK"][0:16, mem["bg1_col"]:mem["bg1_col"] + 1],
             out_slice=bank.ap[0:16, 0:512])
    g1_tok = (ctx.se, ctx.se.n)
    bm_used_tok = bm_tok
    dyn0 = iot0

    for h in range(16):
        ar.off = dyn0
        KF = ar.a(T)
        QF = ar.a(T)
        VT = ar.a(16 * 64)
        V32 = ar.a(16 * 64)
        KTM = ar.a(32 * 64)
        AL = ar.a32(T)
        CP = ar.a32(T)
        D0 = ar.a32(T)
        D1 = ar.a32(T)
        ATL = ar.a32(T)
        WS = ar.a(8 * 64)
        WG2S = ar.a(64)
        SMS = ar.a(32)
        SBF = ar.a(64)
        SST = ar.a32(64)
        CT = ar.a32(64)
        RS = ar.a32(64)
        NBC = ar.a32(64)
        TMP = ar.a32(64)
        BVB = ar.a32(64)
        c0 = h * 64
        barrier(p, ctx)
        for which, dst, wd, bcol in [("q", QF, W["Wq"], bias["bqh"]), ("k", KF, W["Wk"], bias["bkh"])]:
            w_tok = p.ins("gpsimd", "dma_start", out=WS.rearrange("p (k m) -> p k m", k=8),
                          in_=wd[:, c0:c0 + 64].rearrange("(k p) m -> p k m", p=128),
                          waits=[(ctx.te, ctx.te.n)], inc=ctx.dma[0])
            p.wait("tensor", [w_tok, mem["in_tok"]])
            for it in range(4):
                bank = ctx.banks[it % 2]
                p.wait("tensor", bank.ev)
                for k in range(8):
                    p.ins("tensor", "matmul", bank.ap[0:64, 0:512], WS[:, k * 64:(k + 1) * 64],
                          XN[:, k * T + it * 512: k * T + (it + 1) * 512],
                          start=(k == 0), stop=(k == 7), inc=(None if k < 7 else ctx.te))
                evac(p, ctx, bank, dst[0:64, it * 512:(it + 1) * 512], func=AF.Identity,
                     bias=PKH[0:64, bcol + h: bcol + h + 1], out_slice=bank.ap[0:64, 0:512])
        wv_tok = p.ins("gpsimd", "dma_start", out=WS.rearrange("p (k m) -> p k m", k=8),
                       in_=W["Wv"][:, c0:c0 + 64].rearrange("(k p) m -> p k m", p=128),
                       waits=[(ctx.te, ctx.te.n)], inc=ctx.dma[0])
        b5 = ctx.banks[4]
        p.wait("tensor", [bv_tok, b5.ev])
        p.ins("tensor", "matmul", b5.ap[:, 0:64], ONES[0:1, :], BVR[0:1, c0:c0 + 64],
              start=True, stop=True, inc=ctx.te)
        b5.ev = evac(p, ctx, b5, BVB[:, 0:64], out_slice=b5.ap[:, 0:64])
        p.wait("tensor", wv_tok)
        for mi in range(16):
            bank = ctx.banks[mi % 2]
            p.wait("tensor", bank.ev)
            for k in range(8):
                p.ins("tensor", "matmul", bank.ap[:, 0:64],
                      XN[:, k * T + mi * 128: k * T + (mi + 1) * 128],
                      WS[:, k * 64:(k + 1) * 64],
                      start=(k == 0), stop=(k == 7), inc=(None if k < 7 else ctx.te))
            p.wait("vector", [(ctx.te, ctx.te.n), b5.ev])
            bank.ev = p.ins("vector", "tensor_tensor", VT[:, mi * 64:(mi + 1) * 64],
                            bank.ap[:, 0:64], BVB[:, 0:64], op=ALU.add, inc=ctx.ve)
        vt_tok = (ctx.ve, ctx.ve.n)
        for mi in range(16):
            p.ins("vector", "tensor_copy", V32[0:64, mi * 64:(mi + 1) * 64],
                  VT[64:128, mi * 64:(mi + 1) * 64], inc=(ctx.ve if mi == 15 else None))
        v32_tok = (ctx.ve, ctx.ve.n)
        wg2_tok = p.ins("gpsimd", "dma_start", out=WG2S[0:16, 0:64],
                        in_=W["Wg2"][:, c0:c0 + 64], waits=[(ctx.te, ctx.te.n)], inc=ctx.dma[1])
        p.wait("tensor", [wg2_tok, g1_tok])
        for it in range(4):
            bank = ctx.banks[2 + it % 2]
            p.wait("tensor", bank.ev)
            p.ins("tensor", "matmul", bank.ap[0:64, 0:512], WG2S[0:16, 0:64],
                  G1[0:16, it * 512:(it + 1) * 512], start=True, stop=True, inc=ctx.te)
            tok = evac(p, ctx, bank, AL[0:64, it * 512:(it + 1) * 512], func=AF.Sigmoid,
                       bias=PKH[0:64, bias["bg2h"] + h: bias["bg2h"] + h + 1],
                       out_slice=bank.ap[0:64, 0:512])
            p.wait("vector", tok)
            bank.ev = p.ins("vector", "tensor_scalar", AL[0:64, it * 512:(it + 1) * 512],
                            AL[0:64, it * 512:(it + 1) * 512], 0.9, 0.1,
                            op0=ALU.mult, op1=ALU.add, inc=ctx.ve)
        al_tok = (ctx.ve, ctx.ve.n)
        za_tok = p.ins("vector", "memset", ATL[0:64, :], 0.0, inc=ctx.ve)
        PTI = SMS
        PTI = ar.a(64)
        for c in range(T // 64):
            t0 = c * 64
            pb = 64 * (c % 2)
            bsc = ctx.banks[2 + c % 2]
            p.wait("tensor", bsc.ev)
            p.ins("tensor", "matmul", bsc.ap[0:64, 0:64],
                  KF[0:64, t0:t0 + 64], QF[0:64, t0:t0 + 64],
                  start=True, stop=True, inc=ctx.te)
            etok = evac(p, ctx, bsc, PTI[pb:pb + 64, 0:64], func=AF.Exp, scale=0.125,
                        bias=SCAL[0:64, SCAL_COLS["m10"]:SCAL_COLS["m10"] + 1],
                        out_slice=bsc.ap[0:64, 0:64])
            p.wait("vector", etok)
            etok = p.ins("vector", "tensor_tensor", PTI[pb:pb + 64, 0:64], PTI[pb:pb + 64, 0:64],
                         mem["CMP64"][pb:pb + 64, 0:64], op=ALU.mult, inc=ctx.ve)
            bsc.ev = etok
            p.wait("tensor", etok)
            p.ins("tensor", "matmul", bsc.ap[0:1, 64:128], ONESB[pb:pb + 64, 0:1], PTI[pb:pb + 64, 0:64],
                  start=True, stop=True, inc=ctx.te, tile_position=(pb, 0))
            p.wait("vector", (ctx.te, ctx.te.n))
            vfence(p, ctx, "reciprocal", RS[0:1, 0:64], bsc.ap[0:1, 64:128])
            p.wait("tensor", (ctx.ve, ctx.ve.n))
            p.ins("tensor", "matmul", bsc.ap[0:64, 64:128], ONES[0:1, 0:64], RS[0:1, 0:64],
                  start=True, stop=True, inc=ctx.te)
            p.wait("vector", (ctx.te, ctx.te.n))
            bsc.ev = vfence(p, ctx, "tensor_copy", NBC[0:64, 0:64], bsc.ap[0:64, 64:128])
            bpv = ctx.banks[4 + c % 2]
            p.wait("tensor", bpv.ev)
            p.ins("tensor", "matmul", bpv.ap[0:64, 0:64],
                  VT[pb:pb + 64, (t0 // 128) * 64: (t0 // 128) * 64 + 64],
                  PTI[pb:pb + 64, 0:64],
                  start=True, stop=True, inc=ctx.te, tile_position=(pb, 0))
            p.wait("vector", [(ctx.te, ctx.te.n), bsc.ev, za_tok])
            p.ins("vector", "tensor_tensor", TMP[0:64, 0:64], bpv.ap[0:64, 0:64],
                  NBC[0:64, 0:64], op=ALU.mult)
            asl = ATL[0:64, t0:t0 + 64]
            bpv.ev = p.ins("vector", "tensor_tensor", asl, asl, TMP[0:64, 0:64], op=ALU.add, inc=ctx.ve)
        p.wait("vector", [al_tok, bm_used_tok])
        p.ins("vector", "tensor_tensor", D1[0:64, :], AL[0:64, :], BM[0:64, :], op=ALU.mult)
        p.ins("vector", "tensor_tensor", D0[0:64, :], AL[0:64, :], D1[0:64, :], op=ALU.subtract)
        t_ = p.ins("vector", "tensor_tensor_scan", CP[0:64, :], D0[0:64, :], D1[0:64, :], 1.0,
                   op0=ALU.mult, op1=ALU.add, inc=ctx.ve)
        p.wait("vector", t_)
        p.ins("vector", "tensor_copy", D0[0:64, 1:T], CP[0:64, 0:T - 1])
        p.ins("vector", "memset", D0[0:64, 0:1], 1.0)
        p.ins("vector", "tensor_tensor", D1[0:64, :], D0[0:64, :], BM[0:64, :], op=ALU.mult)
        p.ins("vector", "tensor_tensor", D0[0:64, :], D0[0:64, :], D1[0:64, :], op=ALU.subtract)
        p.ins("vector", "tensor_tensor", D0[0:64, :], D0[0:64, :], BM[0:64, :], op=ALU.add)
        p.ins("vector", "tensor_copy", CT[0:64, 0:64], CP[0:64, :].rearrange("p (g s) -> p g s", s=32)[:, :, 31])
        p.ins("vector", "tensor_tensor", QF[0:64, :], QF[0:64, :], D0[0:64, :], op=ALU.mult)
        p.ins("vector", "reciprocal", CP[0:64, :], CP[0:64, :])
        sc_tok = p.ins("vector", "tensor_tensor", KF[0:64, :], KF[0:64, :], CP[0:64, :], op=ALU.mult, inc=ctx.ve)
        tbk = ctx.tbanks[0]
        p.wait("tensor", sc_tok)
        for m in range(32):
            p.wait("tensor", tbk.ev)
            p.ins("tensor", "transpose", tbk.ap[0:64, 0:64], KF[0:64, m * 64:(m + 1) * 64],
                  mem["IDENT"][0:64, 0:64], inc=ctx.te)
            tbk.ev = evac(p, ctx, tbk, KTM[0:64, m * 64:(m + 1) * 64], out_slice=tbk.ap[0:64, 0:64])
        ktm_tok = (ctx.ve, ctx.ve.n)
        zs_tok = p.ins("vector", "memset", SST[0:64, 0:64], 0.0, inc=ctx.ve)
        p.ins("vector", "memset", SMS[0:64, 0:32], 0.0)
        zb_tok = p.ins("vector", "tensor_copy", SBF[0:64, 0:64], SST[0:64, 0:64], inc=ctx.ve)
        p.wait("tensor", [ktm_tok, v32_tok, zb_tok])
        bA = [ctx.banks[2], ctx.banks[3]]
        bB = ctx.banks[4]
        bD = ctx.banks[5]
        for g in range(T // 32):
            ba = bA[g % 2]
            p.wait("tensor", ba.ev)
            p.ins("tensor", "matmul", ba.ap[0:32, 0:32],
                  KF[0:64, g * 32:(g + 1) * 32], QF[0:64, g * 32:(g + 1) * 32],
                  start=True, stop=True, inc=ctx.te)
            vr = 32 * (g % 2)
            p.wait("vector", (ctx.te, ctx.te.n))
            ba.ev = p.ins("vector", "tensor_tensor", SMS[0:32, 0:32] if vr == 0 else SMS[32:64, 0:32],
                          ba.ap[0:32, 0:32], mem["SLT32"], op=ALU.mult, inc=ctx.ve)
            zo_tok = p.ins("vector", "memset", SMS[32:64, 0:32] if vr == 0 else SMS[0:32, 0:32], 0.0, inc=ctx.ve)
            m = g // 4
            vsrc = VT if (g % 4) < 2 else V32
            vcol = m * 64
            p.wait("tensor", [bB.ev, zo_tok])
            p.ins("tensor", "matmul", bB.ap[0:64, 0:32], SBF[0:64, 0:64],
                  QF[0:64, g * 32:(g + 1) * 32], start=True, stop=False)
            p.ins("tensor", "matmul", bB.ap[0:64, 0:32], vsrc[0:64, vcol:vcol + 64],
                  SMS[0:64, 0:32], start=False, stop=True, inc=ctx.te)
            ktcol = (g // 2) * 64
            krow = 32 * (g % 2)
            p.wait("tensor", bD.ev)
            p.ins("tensor", "matmul", bD.ap[0:64, 0:64],
                  KTM[krow:krow + 32, ktcol:ktcol + 64],
                  (VT if (g % 4) < 2 else V32)[(32 * (g % 4) if (g % 4) < 2 else 32 * ((g % 4) - 2)):
                                               (32 * (g % 4) if (g % 4) < 2 else 32 * ((g % 4) - 2)) + 32,
                                               vcol:vcol + 64],
                  start=True, stop=True, inc=ctx.te)
            p.wait("vector", (ctx.te, ctx.te.n))
            asl = ATL[0:64, g * 32:(g + 1) * 32]
            bB.ev = p.ins("vector", "tensor_tensor", asl, asl, bB.ap[0:64, 0:32], op=ALU.add, inc=ctx.ve)
            p.ins("vector", "tensor_tensor", SST[0:64, 0:64], SST[0:64, 0:64], bD.ap[0:64, 0:64], op=ALU.add)
            bD.ev = p.ins("vector", "tensor_scalar", SST[0:64, 0:64], SST[0:64, 0:64],
                          CT[0:64, g:g + 1], None, op0=ALU.mult, inc=ctx.ve)
            sb_tok = p.ins("vector", "tensor_copy", SBF[0:64, 0:64], SST[0:64, 0:64], inc=ctx.ve)
            p.wait("tensor", sb_tok)
        xr = 64 * (h % 2)
        cchunk = h // 2
        for it in range(4):
            sl = slice(it * 512, (it + 1) * 512)
            if xr == 0:
                tok = p.ins("vector", "tensor_tensor", X[0:64, cchunk * T + it * 512: cchunk * T + (it + 1) * 512],
                            X[0:64, cchunk * T + it * 512: cchunk * T + (it + 1) * 512],
                            ATL[0:64, sl], op=ALU.add, inc=ctx.ve)
            else:
                p.ins("vector", "tensor_copy", TMP.bitcast(F32)[64:128, 0:64] if False else D1[64:128, it * 512:(it + 1) * 512],
                      ATL[0:64, sl])
                tok = p.ins("vector", "tensor_tensor", X[64:128, cchunk * T + it * 512: cchunk * T + (it + 1) * 512],
                            X[64:128, cchunk * T + it * 512: cchunk * T + (it + 1) * 512],
                            D1[64:128, it * 512:(it + 1) * 512], op=ALU.add, inc=ctx.ve)
    return (ctx.ve, ctx.ve.n)


def emit_l0_ffn(p, ctx, mem, W, T=2048):
    ar = mem["arena"]
    X, XN, A_bc, B_bc = mem["X"], mem["XN"], mem["A_bc"], mem["B_bc"]
    ONES, ONESB, SCAL = mem["ONES"], mem["ONESB"], mem["SCAL"]
    NIT = 40
    ar.reset()
    PRB = ar.a32(128)
    SLOTM = ar.a32(128)
    PMT = ar.a32(128)
    HIR = ar.a(T)
    LOR = ar.a(T)
    PMR = ar.a(T)
    ROWSB = ar.a32(64)
    dyn0 = ar.off
    XF = ar.a32(8 * 512)
    GW = ar.a32(8 * 8)
    THB = ar.a32(8)
    GEB = ar.a(128)
    CROW = ar.a32(128)
    MR = ar.a32(16)
    barrier(p, ctx)
    gw_tok = p.ins("sync", "dma_start", out=GW.rearrange("p (k m) -> p k m", k=8),
                   in_=W["gw"].rearrange("(k p) m -> p k m", p=128), inc=ctx.dma[2])
    bank = ctx.banks[5]
    for tc in range(16):
        it4, j = tc // 4, tc % 4
        if j == 0:
            p.wait("vector", (ctx.te, ctx.te.n))
            xf_tok = xnf32_tile(p, ctx, X, A_bc, B_bc, XF, T, it4, w=512)
        p.wait("tensor", [gw_tok, xf_tok, bank.ev])
        for k in range(8):
            p.ins("tensor", "matmul", bank.ap[0:128, 0:8],
                  XF[:, k * 512 + j * 128: k * 512 + (j + 1) * 128],
                  GW[:, k * 8:(k + 1) * 8], start=(k == 0), stop=(k == 7),
                  inc=(None if k < 7 else ctx.te))
        p.wait("vector", (ctx.te, ctx.te.n))
        vfence(p, ctx, "tensor_reduce", MR[:, 0:1], bank.ap[0:128, 0:8], axis=mybir.AxisListType.X,
               op=ALU.max, negate=True)
        p.wait("scalar", (ctx.ve, ctx.ve.n))
        et = p.ins("scalar", "activation", PRB[:, tc * 8:(tc + 1) * 8], bank.ap[0:128, 0:8],
                   AF.Exp, bias=MR[:, 0:1], accum_out=MR[:, 1:2], inc=ctx.se)
        bank.ev = et
        p.wait("vector", et)
        vfence(p, ctx, "reciprocal", MR[:, 1:2], MR[:, 1:2])
        vfence(p, ctx, "tensor_scalar", PRB[:, tc * 8:(tc + 1) * 8], PRB[:, tc * 8:(tc + 1) * 8],
               MR[:, 1:2], None, op0=ALU.mult)
    lo, hi, th, cnt = ROWSB[0:1, 0:8], ROWSB[0:1, 8:16], ROWSB[0:1, 16:24], ROWSB[0:1, 24:32]
    nc_ = ROWSB[0:1, 32:40]
    d_ = ROWSB[0:1, 40:48]
    p.ins("vector", "memset", lo, 0.0)
    p.ins("vector", "memset", hi, 1.0)
    t_ = p.ins("vector", "memset", th, 0.5, inc=ctx.ve)
    p.wait("vector", t_)
    b4 = ctx.banks[4]
    for itn in range(NIT + 1):
        final = (itn == NIT)
        src_row = lo if final else th
        p.wait("tensor", [b4.ev, (ctx.ve, ctx.ve.n)])
        p.ins("tensor", "matmul", b4.ap[:, 0:8], ONES[0:1, :], src_row, start=True, stop=True, inc=ctx.te)
        p.wait("vector", (ctx.te, ctx.te.n))
        b4.ev = vfence(p, ctx, "tensor_copy", THB[:, 0:8], b4.ap[:, 0:8])
        t_ = vfence(p, ctx, "tensor_tensor", GEB.rearrange("p (c e) -> p c e", e=8),
                    PRB.rearrange("p (c e) -> p c e", e=8),
                    THB[:, 0:8].rearrange("p (o e) -> p o e", o=1).to_broadcast([128, 16, 8]),
                    op=ALU.is_ge)
        if final:
            break
        p.wait("tensor", [t_, b4.ev])
        p.ins("tensor", "matmul", b4.ap[0:1, 0:128], ONESB[:, 0:1], GEB, start=True, stop=True, inc=ctx.te)
        p.wait("vector", (ctx.te, ctx.te.n))
        vfence(p, ctx, "tensor_copy", CROW[0:1, 0:128], b4.ap[0:1, 0:128])
        b4.ev = (ctx.ve, ctx.ve.n)
        vfence(p, ctx, "tensor_reduce", cnt, CROW[0:1, 0:128].rearrange("o (c e) -> o e c", e=8),
               axis=mybir.AxisListType.X, op=ALU.add)
        vfence(p, ctx, "tensor_scalar", cnt, cnt, 512.0, None, op0=ALU.is_ge)
        vfence(p, ctx, "tensor_tensor", d_, th, lo, op=ALU.subtract)
        vfence(p, ctx, "tensor_tensor", d_, d_, cnt, op=ALU.mult)
        vfence(p, ctx, "tensor_tensor", lo, lo, d_, op=ALU.add)
        vfence(p, ctx, "tensor_scalar", nc_, cnt, -1.0, 1.0, op0=ALU.mult, op1=ALU.add)
        vfence(p, ctx, "tensor_tensor", d_, th, hi, op=ALU.subtract)
        vfence(p, ctx, "tensor_tensor", d_, d_, nc_, op=ALU.mult)
        vfence(p, ctx, "tensor_tensor", hi, hi, d_, op=ALU.add)
        vfence(p, ctx, "tensor_tensor", th, lo, hi, op=ALU.add)
        vfence(p, ctx, "tensor_scalar", th, th, 0.5, None, op0=ALU.mult)
    mask_tok = t_
    b5 = ctx.banks[5]
    for c in range(16):
        p.wait("tensor", [mask_tok, b5.ev])
        for cp in range(c + 1):
            lh = mem["TRIU"] if cp == c else ONESB
            p.ins("tensor", "matmul", b5.ap[0:128, 0:8], lh,
                  GEB[:, cp * 8:(cp + 1) * 8], start=(cp == 0), stop=(cp == c), inc=ctx.te)
        p.wait("vector", (ctx.te, ctx.te.n))
        b5.ev = p.ins("vector", "tensor_copy", SLOTM[:, c * 8:(c + 1) * 8], b5.ap[0:128, 0:8], inc=ctx.ve)
    p.ins("vector", "tensor_tensor", PMT, PRB, GEB, op=ALU.mult)
    p.ins("vector", "tensor_tensor", SLOTM, SLOTM, GEB, op=ALU.mult)
    sm_tok = p.ins("vector", "tensor_scalar", SLOTM, SLOTM, 1.0, None, op0=ALU.subtract, inc=ctx.ve)
    HI = ar.a32(128)
    LO = ar.a32(128)
    HIB_ = ar.a(128)
    LOB_ = ar.a(128)
    PMB_ = ar.a(128)
    IT128 = ar.a32(128).bitcast(I32)
    p.wait("vector", sm_tok)
    vfence(p, ctx, "tensor_scalar", HI, SLOTM, 1.0 / 128, None, op0=ALU.mult)
    vfence(p, ctx, "tensor_copy", IT128[:, 0:128], HI)
    vfence(p, ctx, "tensor_copy", LO, IT128[:, 0:128])
    ADJ = ar.a32(128)
    vfence(p, ctx, "tensor_tensor", ADJ, LO, HI, op=ALU.is_gt)
    vfence(p, ctx, "tensor_tensor", HI, LO, ADJ, op=ALU.subtract)
    vfence(p, ctx, "tensor_scalar", LO, HI, -128.0, None, op0=ALU.mult)
    vfence(p, ctx, "tensor_tensor", LO, SLOTM, LO, op=ALU.add)
    p.ins("vector", "tensor_copy", HIB_, HI)
    p.ins("vector", "tensor_copy", LOB_, LO)
    t_ = p.ins("vector", "tensor_copy", PMB_, PMT, inc=ctx.ve)
    p.wait("tensor", t_)
    tbx = ctx.tbanks[1]
    for c in range(16):
        for srcb, dstr in [(HIB_, HIR), (LOB_, LOR), (PMB_, PMR)]:
            p.wait("tensor", tbx.ev)
            p.ins("tensor", "transpose", tbx.ap[0:8, 0:128], srcb[:, c * 8:(c + 1) * 8],
                  mem["IDENT"], inc=ctx.te)
            tbx.ev = evac(p, ctx, tbx, dstr[0:8, c * 128:(c + 1) * 128], out_slice=tbx.ap[0:8, 0:128])
    rows_tok = (ctx.ve, ctx.ve.n)
    ar.off = dyn0
    XNT = ar.a(16 * 1024)
    SELTC = ar.a(512)
    XINE = ar.a(8 * 512)
    barrier(p, ctx)
    tbk = ctx.tbanks[0]
    for c in range(8):
        for i in range(16):
            p.wait("tensor", tbk.ev)
            p.ins("tensor", "transpose", tbk.ap[0:128, 0:128], XN[:, c * T + i * 128: c * T + (i + 1) * 128],
                  mem["IDENT"], inc=ctx.te)
            tbk.ev = evac(p, ctx, tbk, XNT[:, i * 1024 + c * 128: i * 1024 + (c + 1) * 128],
                          out_slice=tbk.ap[0:128, 0:128])
    xnt_tok = (ctx.ve, ctx.ve.n)
    allb = ctx.banks + ctx.tbanks
    for e in range(8):
        for i in range(16):
            p.wait("vector", (ctx.te, ctx.te.n))
            st_ = p.ins("vector", "tensor_scalar", SELTC, mem["IOTA512"],
                        SLOTM[:, i * 8 + e: i * 8 + e + 1], None, op0=ALU.is_equal, inc=ctx.ve)
            p.wait("tensor", [st_, xnt_tok])
            for dm in range(8):
                bk = allb[dm]
                if i == 0:
                    p.wait("tensor", bk.ev)
                out = bk.ap if dm < 6 else bk.ap.bitcast(F32)
                p.ins("tensor", "matmul", out[0:128, 0:512],
                      XNT[:, i * 1024 + dm * 128: i * 1024 + (dm + 1) * 128],
                      SELTC, start=(i == 0), stop=(i == 15),
                      inc=(ctx.te if dm == 7 else None))
        p.wait("vector", (ctx.te, ctx.te.n))
        for dm in range(8):
            bk = allb[dm]
            out = bk.ap if dm < 6 else bk.ap.bitcast(F32)
            bk.ev = p.ins("vector", "tensor_copy", XINE[:, dm * 512:(dm + 1) * 512],
                          out[0:128, 0:512], inc=ctx.ve)
        xw_tok = p.ins("gpsimd", "dma_start", out=W["xind"][e], in_=XINE,
                       waits=[(ctx.ve, ctx.ve.n), (ctx.dma[3], ctx.dma[3].n)], inc=ctx.dma[3])
    ar.off = dyn0
    XIN = ar.a(8 * 512)
    W1S = ar.a(8 * 512)
    W2S = ar.a(4 * 1024)
    HS = ar.a(4 * 512)
    Y32 = ar.a32(4 * 1024)
    YT = ar.a(4 * 1024)
    SBCT = ar.a32(512)
    PBCT = ar.a32(512)
    PBC2 = ar.a32(512)
    SELP = ar.a(4 * 512)
    barrier(p, ctx)
    for e in range(8):
        xr_tok = p.ins("gpsimd", "dma_start", out=XIN, in_=W["xind"][e],
                       waits=[(ctx.te, ctx.te.n), (ctx.dma[3], ctx.dma[3].n)], inc=ctx.dma[3])
        for s in range(8):
            w1_tok = p.ins("gpsimd", "dma_start", out=W1S.rearrange("p (k m) -> p k m", k=8),
                           in_=W["W1"][e][:, s * 512:(s + 1) * 512].rearrange("(k p) m -> p k m", p=128),
                           waits=[(ctx.te, ctx.te.n)], inc=ctx.dma[0])
            w2_tok = p.ins("gpsimd", "dma_start", out=W2S.rearrange("p (k m) -> p k m", k=4),
                           in_=W["W2"][e][s * 512:(s + 1) * 512, :].rearrange("(k p) m -> p k m", p=128),
                           waits=[(ctx.te, ctx.te.n)], inc=ctx.dma[1])
            p.wait("tensor", [w1_tok, xr_tok])
            for mi in range(4):
                bk = ctx.banks[mi % 2]
                p.wait("tensor", bk.ev)
                for k in range(8):
                    p.ins("tensor", "matmul", bk.ap[:, 0:512],
                          W1S[:, k * 512 + mi * 128: k * 512 + (mi + 1) * 128],
                          XIN[:, k * 512:(k + 1) * 512],
                          start=(k == 0), stop=(k == 7), inc=(None if k < 7 else ctx.te))
                evac(p, ctx, bk, HS[:, mi * 512:(mi + 1) * 512], func=AF.Gelu,
                     out_slice=bk.ap[:, 0:512])
            h_tok = (ctx.se, ctx.se.n)
            p.wait("tensor", [w2_tok, h_tok])
            for sm in range(4):
                for dn in range(2):
                    bk = ctx.banks[2 + (sm * 2 + dn) % 2]
                    p.wait("tensor", bk.ev)
                    for k in range(4):
                        p.ins("tensor", "matmul", bk.ap[:, 0:512],
                              HS[:, k * 512 + sm * 128: k * 512 + (sm + 1) * 128],
                              W2S[:, k * 1024 + dn * 512: k * 1024 + (dn + 1) * 512],
                              start=(k == 0), stop=(k == 3), inc=(None if k < 3 else ctx.te))
                    ysl = Y32[:, sm * 1024 + dn * 512: sm * 1024 + (dn + 1) * 512]
                    if s == 0:
                        evac(p, ctx, bk, ysl, out_slice=bk.ap[:, 0:512])
                    else:
                        evac(p, ctx, bk, ysl, add_to=ysl, out_slice=bk.ap[:, 0:512])
        for sm in range(4):
            p.ins("vector", "tensor_copy", YT[:, sm * 1024:(sm + 1) * 1024],
                  Y32[:, sm * 1024:(sm + 1) * 1024], inc=(ctx.ve if sm == 3 else None))
        yt_tok = (ctx.ve, ctx.ve.n)
        for tt in range(4):
            t0 = tt * 512
            b4 = ctx.banks[4]
            p.wait("tensor", [rows_tok, b4.ev])
            p.ins("tensor", "matmul", b4.ap[:, 0:512], mem["OH8B"][0:8, e * 128:(e + 1) * 128],
                  HIR[0:8, t0:t0 + 512], start=True, stop=True, inc=ctx.te)
            b4.ev = evac(p, ctx, b4, SBCT, out_slice=b4.ap[:, 0:512])
            b5 = ctx.banks[5]
            p.wait("tensor", b5.ev)
            p.ins("tensor", "matmul", b5.ap[:, 0:512], mem["OH8B"][0:8, e * 128:(e + 1) * 128],
                  LOR[0:8, t0:t0 + 512], start=True, stop=True, inc=ctx.te)
            b5.ev = evac(p, ctx, b5, PBCT, out_slice=b5.ap[:, 0:512])
            p.wait("tensor", [b4.ev, b5.ev])
            p.ins("tensor", "matmul", b4.ap[:, 0:512], mem["OH8B"][0:8, e * 128:(e + 1) * 128],
                  PMR[0:8, t0:t0 + 512], start=True, stop=True, inc=ctx.te)
            pb_tok = evac(p, ctx, b4, PBC2, out_slice=b4.ap[:, 0:512])
            b4.ev = pb_tok
            p.wait("vector", [pb_tok, b5.ev])
            for sc in range(4):
                p.ins("vector", "tensor_scalar", SELP[:, sc * 512:(sc + 1) * 512], PBCT,
                      mem["IOTSC"][:, sc:sc + 1] if False else None, None, op0=ALU.bypass) if False else None
                p.ins("vector", "tensor_scalar", SELP[:, sc * 512:(sc + 1) * 512], SBCT,
                      float(sc), None, op0=ALU.is_equal)
                p.ins("vector", "scalar_tensor_tensor", SELP[:, sc * 512:(sc + 1) * 512], PBCT,
                      mem["IOTAC"][:, 0:1], SELP[:, sc * 512:(sc + 1) * 512],
                      op0=ALU.is_equal, op1=ALU.mult)
                p.ins("vector", "tensor_tensor", SELP[:, sc * 512:(sc + 1) * 512],
                      SELP[:, sc * 512:(sc + 1) * 512], PBC2, op=ALU.mult,
                      inc=(ctx.ve if sc == 3 else None))
            sp_tok = (ctx.ve, ctx.ve.n)
            p.wait("tensor", [sp_tok, yt_tok])
            for dm in range(8):
                bk = ctx.banks[dm % 2]
                p.wait("tensor", bk.ev)
                for sc in range(4):
                    p.ins("tensor", "matmul", bk.ap[:, 0:512],
                          YT[:, sc * 1024 + dm * 128: sc * 1024 + (dm + 1) * 128],
                          SELP[:, sc * 512:(sc + 1) * 512],
                          start=(sc == 0), stop=(sc == 3), inc=(None if sc < 3 else ctx.te))
                xs = X[:, dm * T + t0: dm * T + t0 + 512]
                evac(p, ctx, bk, xs, add_to=xs, out_slice=bk.ap[:, 0:512])
    return (ctx.ve, ctx.ve.n)

T_SEQ = 2048
N_CORES = 8
SEQS_PER_CORE = 2
ABASE = 8192
FUSED = False

AR_COLS = {0: 54272, 1: 52864, 2: 50432, 3: 53568}

_COMMON_CONSTS = [
    ("ONES", "c_ones", "F32", [128, 128], "s"),
    ("ONESB", "c_ones", "BF16", [128, 128], "g"),
    ("SCAL", "c_scal", "F32", [128, 32], "s"),
    ("IDENT", "c_ident", "BF16", [128, 128], "g"),
]
_LAYER_CONSTS = {
    0: [("PACK", "pk_main", "F32", [128, 96], "s"),
        ("TRIU", "c_triu", "BF16", [128, 128], "g"),
        ("OH8B", "c_oh8", "BF16", [8, 1024], "g"),
        ("IOTA512", "c_iota512", "F32", [128, 512], "s"),
        ("IOTAC", "c_iotac", "F32", [128, 4], "s")],
    1: [("PACK", "pk_main", "F32", [128, 96], "s"),
        ("PKH", "pk_head", "F32", [64, 48], "s"),
        ("CMP64", "c_cmp64", "BF16", [128, 64], "g"),
        ("SLT32", "c_slt32", "BF16", [32, 32], "g"),
        ("OH4", "c_oh4", "BF16", [4, 512], "g"),
        ("IOTA4C", "c_iota4c", "F32", [4, 4], "s"),
        ("B1P1", "b1p_l1", "F32", [128, 128], "s"),
        ("B2P1", "b2p_l1", "F32", [128, 32], "s")],
    2: [("P0", "c_p0", "F32", [128, 256], "s"),
        ("P0M0", "c_p0m0", "F32", [128, 256], "s"),
        ("P0M1", "c_p0m1", "F32", [128, 256], "s"),
        ("ASLP", "c_aslp", "F32", [128, 16], "s"),
        ("AOFF", "c_aoff", "F32", [128, 272], "s"),
        ("OH32", "c_oh32", "BF16", [32, 4096], "g")],
    3: [("CM0", "c_cm0h", "BF16", [128, 128], "g"),
        ("OH4", "c_oh4", "BF16", [4, 512], "g"),
        ("B1P3", "b1p_l3", "F32", [128, 128], "s"),
        ("B2P3", "b2p_l3", "F32", [128, 32], "s")],
}
_LAYER_SCRATCH = {
    0: [],
    1: [("OHF", "BF16", [4, T_SEQ]), ("IT256", "I32", [1, 256])],
    2: [],
    3: [("OHF", "BF16", [4, T_SEQ]), ("OHTM", "BF16", [128, 64])],
}
_WEIGHTS = {
    0: [("l0_wqkv", [1024, 3072]), ("l0_wot", [1024, 1024]), ("l0_gw", [1024, 8]),
        ("l0_w1", [8, 1024, 4096]), ("l0_w2", [8, 4096, 1024])],
    1: [("l1_wq", [1024, 1024]), ("l1_wk", [1024, 1024]), ("l1_wv", [1024, 1024]),
        ("l1_wg1", [1024, 16]), ("l1_wg2", [16, 1024]), ("l1_bv", [1024]),
        ("l1f_wq", [1024, 1024]), ("l1f_wk", [1024, 1024]), ("l1f_wv", [1024, 1024]),
        ("l1f_wo", [1024, 1024]), ("l1f_rw", [1024, 1]), ("l1f_rb", [1]),
        ("l1f_w1", [4, 1024, 4096]), ("l1f_w2", [4, 4096, 1024])],
    2: [("l2_wattn", [1024, 2048]), ("l2_vk", [1024, 1024]), ("l2_pk", [1024, 1024]),
        ("l2f_gwt", [1024, 32]), ("l2f_gb", [32]),
        ("l2f_wg", [32, 1024, 512]), ("l2f_wu", [32, 1024, 512]), ("l2f_wd", [32, 512, 1024]),
        ("l2f_shg", [1024, 1024]), ("l2f_shu", [1024, 1024]), ("l2f_shd", [1024, 1024])],
    3: [("l3_wq", [1024, 1024]), ("l3_wk", [1024, 1024]), ("l3_wv", [1024, 1024]),
        ("l3_wo", [1024, 1024]),
        ("l3f_w1", [4, 1024, 4096]), ("l3f_w2", [4, 4096, 1024]),
        ("l3f_wg", [1024, 4]), ("l3f_bg", [4])],
}


def _dt(s):
    return {"F32": F32, "BF16": BF16, "I32": I32}[s]


PKC = {"l0_bout": 0, "l1_bg1": 8,
       "f_bq": 16, "f_bk": 24, "f_bv": 32, "f_bo": 40, "f_g1": 48, "f_b1ln": 56,
       "f_g2": 64, "f_b2ln": 72}


def emit_layer(p, ctx, mem, d, li, in_toks):
    ar = mem["arena"]
    ar.base = 0
    ar.off = 0
    A = ar.a32(T_SEQ)
    B = ar.a32(T_SEQ)
    assert ar.off == ABASE
    ar.base = ABASE
    mem["A_bc"], mem["B_bc"] = A, B
    stage_reset(p, ctx, ar, ABASE)
    SQ = ar.a32(8 * 512)
    TMPB = ar.a32(512)
    ROWS = ar.a32(4 * 512)
    ln_tok = layernorm(p, ctx, mem["X"], mem["XN"], A, B, T_SEQ,
                       mem["ONES"], SQ, ROWS, in_toks, scal=mem["SCAL"], tmpbuf=TMPB)
    mem["in_tok"] = ln_tok
    stage_reset(p, ctx, ar, ABASE)
    if li == 0:
        W = {"gw": d["l0_gw"], "W1": [d["l0_w1"][e] for e in range(8)],
             "W2": [d["l0_w2"][e] for e in range(8)], "xind": d["xind"]}
        t1 = emit_l0_ffn(p, ctx, mem, W, T=T_SEQ)
        mem["bout_col"] = PKC["l0_bout"]
        t2 = emit_l0_attn(p, ctx, mem, {"WQKV": d["l0_wqkv"], "WOT": d["l0_wot"]}, T=T_SEQ)
    elif li == 1:
        W = {"Wq": d["l1f_wq"], "Wk": d["l1f_wk"], "Wv": d["l1f_wv"], "Wo": d["l1f_wo"],
             "router_w": d["l1f_rw"], "router_b": d["l1f_rb"], "x1d": d["x1d"],
             "W1": [d["l1f_w1"][e] for e in range(4)], "W2": [d["l1f_w2"][e] for e in range(4)]}
        bias = {"pack": mem["PACK"], "bq": PKC["f_bq"], "bk": PKC["f_bk"], "bv": PKC["f_bv"],
                "bo": PKC["f_bo"], "g1": PKC["f_g1"], "b1ln": PKC["f_b1ln"],
                "g2": PKC["f_g2"], "b2ln": PKC["f_b2ln"],
                "b1pack": mem["B1P1"], "b1": 0, "b2pack": mem["B2P1"], "b2": 0}
        t1 = emit_l1_ffn(p, ctx, mem, W, bias, T=T_SEQ)
        mem["bg1_col"] = PKC["l1_bg1"]
        t2 = emit_l1_attn(p, ctx, mem, {"Wq": d["l1_wq"], "Wk": d["l1_wk"], "Wv": d["l1_wv"],
                                        "Wg1": d["l1_wg1"], "Wg2": d["l1_wg2"], "bv": d["l1_bv"]},
                          {"bqh": 0, "bkh": 16, "bg2h": 32}, T=T_SEQ)
    elif li == 2:
        W = {"gwT": d["l2f_gwt"], "gate_bias": d["l2f_gb"],
             "wg": [d["l2f_wg"][e] for e in range(32)], "wu": [d["l2f_wu"][e] for e in range(32)],
             "wd": [d["l2f_wd"][e] for e in range(32)],
             "shG": d["l2f_shg"], "shU": d["l2f_shu"], "shD": d["l2f_shd"]}
        t1 = emit_l2_ffn(p, ctx, mem, W, T=T_SEQ)
        t2 = emit_l2_attn(p, ctx, mem, {"W_attn": d["l2_wattn"], "VK": d["l2_vk"],
                                        "PKron": d["l2_pk"]}, T=T_SEQ)
    else:
        emit_gate_argmax4(p, ctx, mem, d["l3f_wg"], d["l3f_bg"], mem["OHF"],
                          T=T_SEQ, mode="argmax")
        stage_reset(p, ctx, ar, ABASE)
        W = {"W1": [d["l3f_w1"][e] for e in range(4)], "W2": [d["l3f_w2"][e] for e in range(4)]}
        t1 = emit_dense_moe(p, ctx, mem, W, mem["OHF"], T=T_SEQ, n_exp=4, dff=4096,
                            act=AF.Gelu, b1_pack=mem["B1P3"], b1_col0=0,
                            b2_pack=mem["B2P3"], b2_col0=0, dst=mem["X"], in_tok=ln_tok)
        ar.base = 0
        t2 = emit_l3_attn(p, ctx, mem, {"Wq": d["l3_wq"], "Wk": d["l3_wk"], "Wv": d["l3_wv"],
                                        "Wo": d["l3_wo"]}, T=T_SEQ)
        ar.base = ABASE
    return [t1, t2, (ctx.ve, ctx.ve.n), (ctx.se, ctx.se.n)]


def build_program(layers):
    from contextlib import ExitStack
    nc = bass.Bass("TRN2", target_bir_lowering=False)
    es = ExitStack()
    p = Prog(nc, es)
    d = {}

    def di(name, shape, dt=F32):
        d[name] = nc.dram_tensor(name, list(shape), dt, kind="ExternalInput").ap()

    di("xin", [SEQS_PER_CORE, 1024, T_SEQ])
    d["xout"] = nc.dram_tensor("xout", [SEQS_PER_CORE, 1024, T_SEQ], F32,
                               kind="ExternalOutput").ap()
    consts = list(_COMMON_CONSTS)
    have = {c[0] for c in consts}
    for li in layers:
        for c in _LAYER_CONSTS[li]:
            if c[0] not in have:
                consts.append(c)
                have.add(c[0])
    for _, dk, _, sh, _ in consts:
        if dk not in d:
            di(dk, [128, 256] if dk == "c_cm0h" else sh)
    for li in layers:
        for nm, sh in _WEIGHTS[li]:
            di(nm, sh)
    if 0 in layers:
        d["xind"] = nc.dram_tensor("xind", [8, 128, 4096], BF16, kind="Internal").ap()
    if 1 in layers:
        d["x1d"] = nc.dram_tensor("x1d", [1024, T_SEQ], BF16, kind="Internal").ap()

    ctx = Ctx(nc, es, p)
    mem = {}
    mem["X"] = ctx.sb("X", [128, 8 * T_SEQ], F32)
    mem["XN"] = ctx.sb("XN", [128, 8 * T_SEQ], BF16)
    for mk_, dk, dts, sh, q in consts:
        mem[mk_] = ctx.sb(mk_, sh, _dt(dts))
    for li in layers:
        for mk_, dts, sh in _LAYER_SCRATCH[li]:
            if mk_ not in mem:
                mem[mk_] = ctx.sb(mk_, sh, _dt(dts))
    arcols = max(AR_COLS[li] for li in layers)
    AR = ctx.sb("AR", [128, arcols], BF16)
    mem["arena"] = Arena(AR, arcols, base=ABASE)

    for mk_, dk, dts, sh, q in consts:
        src = d[dk][:, 0:128] if dk == "c_cm0h" else d[dk]
        if q == "s":
            p.ins("sync", "dma_start", out=mem[mk_], in_=src, inc=ctx.ldS)
        else:
            p.ins("gpsimd", "dma_start", out=mem[mk_], in_=src, inc=ctx.ldG)
    const_tok = [(ctx.ldS, ctx.ldS.n), (ctx.ldG, ctx.ldG.n)]
    for e in ENGS:
        p.wait(e, const_tok)

    prev_store = None
    for s in range(SEQS_PER_CORE):
        waits = const_tok + ([prev_store] if prev_store else [])
        x_tok = p.ins("sync", "dma_start",
                      out=mem["X"].rearrange("p (c t) -> p c t", c=8),
                      in_=d["xin"][s].rearrange("(c p) t -> p c t", p=128),
                      waits=waits, inc=ctx.sx)
        for e in ENGS:
            p.wait(e, x_tok)
        toks = [x_tok]
        for li in layers:
            toks = emit_layer(p, ctx, mem, d, li, toks)
        p.wait("sync", toks)
        prev_store = p.ins("sync", "dma_start",
                           out=d["xout"][s].rearrange("(c p) t -> p c t", p=128),
                           in_=mem["X"].rearrange("p (c t) -> p c t", c=8), inc=ctx.st)
    p.wait("sync", prev_store)
    p.emit()
    es.close()
    return nc



def prep_shared(inp):
    c = make_consts()
    f32 = np.float32
    A = {}
    for k, v in c.items():
        A[k] = np.ascontiguousarray(v.astype(f32))
    A["c_cm0h"] = A["c_cm0"]
    oh32 = np.zeros((32, 4096), f32)
    for e in range(32):
        oh32[e, e * 128:(e + 1) * 128] = 1.0
    A["c_oh32"] = oh32
    pk = np.zeros((128, 96), f32)

    def col(vec, c0):
        v = np.asarray(vec)
        for j in range(v.shape[0] // 128):
            pk[:, c0 + j] = v[j * 128:(j + 1) * 128]
    col(inp["L0_attn_b_out"], PKC["l0_bout"])
    pk[0:16, PKC["l1_bg1"]] = np.asarray(inp["L1_attn_bg1"])
    for nm, key in [("L1_ffn_bq", "f_bq"), ("L1_ffn_bk", "f_bk"), ("L1_ffn_bv", "f_bv"),
                    ("L1_ffn_bo", "f_bo"), ("L1_ffn_ln1_g", "f_g1"), ("L1_ffn_ln1_b", "f_b1ln"),
                    ("L1_ffn_ln2_g", "f_g2"), ("L1_ffn_ln2_b", "f_b2ln")]:
        col(inp[nm], PKC[key])
    A["pk_main"] = pk
    pkh = np.zeros((64, 48), f32)
    bq = np.asarray(inp["L1_attn_bq"]); bk = np.asarray(inp["L1_attn_bk"])
    bg2 = np.asarray(inp["L1_attn_bg2"])
    for h in range(16):
        pkh[:, 0 + h] = bq[h * 64:(h + 1) * 64]
        pkh[:, 16 + h] = bk[h * 64:(h + 1) * 64]
        pkh[:, 32 + h] = bg2[h * 64:(h + 1) * 64]
    A["pk_head"] = pkh

    def bpack(b, ncol):
        b = np.asarray(b)
        out = np.zeros((128, b.shape[0] * ncol), f32)
        for e in range(b.shape[0]):
            for j in range(ncol):
                out[:, e * ncol + j] = b[e, j * 128:(j + 1) * 128]
        return out
    A["b1p_l1"] = bpack(inp["L1_ffn_b1"], 32)
    A["b2p_l1"] = bpack(inp["L1_ffn_b2"], 8)
    A["b1p_l3"] = bpack(inp["L3_ffn_b1"], 32)
    A["b2p_l3"] = bpack(inp["L3_ffn_b2"], 8)

    ct = np.ascontiguousarray
    A["l0_wqkv"] = ct(np.asarray(inp["L0_attn_w_qkv"]).T)
    A["l0_wot"] = ct(np.asarray(inp["L0_attn_w_out"]).T)
    A["l0_gw"] = ct(np.asarray(inp["L0_ffn_gate_w"]))
    A["l0_w1"] = ct(np.asarray(inp["L0_ffn_w1"]))
    A["l0_w2"] = ct(np.asarray(inp["L0_ffn_w2"]))
    A["l1_wq"] = ct(np.asarray(inp["L1_attn_Wq"]))
    A["l1_wk"] = ct(np.asarray(inp["L1_attn_Wk"]))
    A["l1_wv"] = ct(np.asarray(inp["L1_attn_Wv"]))
    A["l1_wg1"] = ct(np.asarray(inp["L1_attn_Wg1"]))
    A["l1_wg2"] = ct(np.asarray(inp["L1_attn_Wg2"]))
    A["l1_bv"] = ct(np.asarray(inp["L1_attn_bv"]))
    A["l1f_wq"] = ct(np.asarray(inp["L1_ffn_Wq"]))
    A["l1f_wk"] = ct(np.asarray(inp["L1_ffn_Wk"]))
    A["l1f_wv"] = ct(np.asarray(inp["L1_ffn_Wv"]))
    A["l1f_wo"] = ct(np.asarray(inp["L1_ffn_Wo"]))
    A["l1f_rw"] = ct(np.asarray(inp["L1_ffn_router_w"]))
    A["l1f_rb"] = ct(np.asarray(inp["L1_ffn_router_b"]))
    A["l1f_w1"] = ct(np.asarray(inp["L1_ffn_W1"]))
    A["l1f_w2"] = ct(np.asarray(inp["L1_ffn_W2"]))
    A["l2_wattn"] = ct(np.asarray(inp["L2_attn_W_attn"]))
    vt = np.asarray(inp["L2_attn_v_tmp"]); pt = np.asarray(inp["L2_attn_proj_tmp"])
    VK = np.zeros((1024, 1024), f32); PKr = np.zeros((1024, 1024), f32)
    idx = np.arange(64)
    for i in range(16):
        for j in range(16):
            VK[i * 64 + idx, j * 64 + idx] = vt[i, j]
            PKr[j * 64 + idx, i * 64 + idx] = pt[i, j]
    A["l2_vk"] = VK; A["l2_pk"] = PKr
    A["l2f_gwt"] = ct(np.asarray(inp["L2_ffn_gate_w"]).T)
    A["l2f_gb"] = ct(np.asarray(inp["L2_ffn_gate_bias"]))
    A["l2f_wg"] = ct(np.asarray(inp["L2_ffn_w_gate"]).transpose(0, 2, 1))
    A["l2f_wu"] = ct(np.asarray(inp["L2_ffn_w_up"]).transpose(0, 2, 1))
    A["l2f_wd"] = ct(np.asarray(inp["L2_ffn_w_down"]).transpose(0, 2, 1))
    A["l2f_shg"] = ct(np.asarray(inp["L2_ffn_sh_gate"]).T)
    A["l2f_shu"] = ct(np.asarray(inp["L2_ffn_sh_up"]).T)
    A["l2f_shd"] = ct(np.asarray(inp["L2_ffn_sh_down"]).T)
    A["l3_wq"] = ct(np.asarray(inp["L3_attn_Wq"]))
    A["l3_wk"] = ct(np.asarray(inp["L3_attn_Wk"]))
    A["l3_wv"] = ct(np.asarray(inp["L3_attn_Wv"]))
    A["l3_wo"] = ct(np.asarray(inp["L3_attn_Wo"]))
    A["l3f_w1"] = ct(np.asarray(inp["L3_ffn_W1"]))
    A["l3f_w2"] = ct(np.asarray(inp["L3_ffn_W2"]))
    A["l3f_wg"] = ct(np.asarray(inp["L3_ffn_Wg"]))
    A["l3f_bg"] = ct(np.asarray(inp["L3_ffn_bg"]))
    for k in list(A):
        A[k] = np.ascontiguousarray(np.asarray(A[k], dtype=f32))
    return A


def _input_keys(layers):
    ks = set()
    for c in _COMMON_CONSTS:
        ks.add(c[1])
    for li in layers:
        for c in _LAYER_CONSTS[li]:
            ks.add(c[1])
        for nm, _ in _WEIGHTS[li]:
            ks.add(nm)
    return sorted(ks)


_PROG_CACHE = {}


def kernel(**inputs):
    x = np.asarray(inputs["L0_attn_x"], dtype=np.float32)
    A = prep_shared(inputs)
    xsh = [np.ascontiguousarray(x[2 * i:2 * i + 2].transpose(0, 2, 1)) for i in range(N_CORES)]
    groups = [(0,), (1,), (2,), (3,)] if not FUSED else [(0, 1, 2, 3)]
    cur = xsh
    for layers in groups:
        if layers not in _PROG_CACHE:
            _PROG_CACHE[layers] = build_program(list(layers))
        nc = _PROG_CACHE[layers]
        keys = _input_keys(layers)
        in_maps = []
        for i in range(N_CORES):
            m = {k: A[k] for k in keys}
            m["xin"] = cur[i]
            in_maps.append(m)
        res = run_bass_kernel_spmd(nc, in_maps, core_ids=list(range(N_CORES)))
        cur = [np.asarray(res.results[i]["xout"]) for i in range(N_CORES)]
    out = np.empty((16, 2048, 1024), np.float32)
    for i in range(N_CORES):
        out[2 * i:2 * i + 2] = cur[i].transpose(0, 2, 1)
    return out
